# version 17
# baseline (speedup 1.0000x reference)
import contextlib
import numpy as np
import ml_dtypes
import concourse.bass as bass
import concourse.mybir as mybir
from concourse.bass_utils import run_bass_kernel_spmd

F32 = mybir.dt.float32
BF16 = mybir.dt.bfloat16
AF = mybir.ActivationFunctionType
ALU = mybir.AluOpType
AX = mybir.AxisListType

NCORE = 8
SEQ = 2
T = 2048
NT = SEQ * T
C = 1024
H = 16
D = 64
LAT = 256
HID = 2048
E = 8
NTILE = NT // 128
TB = 1024
NTB = NT // TB

BIGNEG = -1.0e30


class Prog:

    def __init__(self, nc, blk, sems, v=None):
        self.nc = nc
        self.blk = blk
        self.sems = sems
        self.v = {k: 0 for k in sems} if v is None else v

    def inc(self, inst, s, n=1):
        inst.then_inc(self.sems[s], n)
        self.v[s] += n
        return self.v[s]

    def dinc(self, eng, s, n=1):
        eng.drain().then_inc(self.sems[s], n)
        self.v[s] += n
        return self.v[s]

    def bump(self, s, n):
        self.v[s] += n
        return self.v[s]


class GC:

    def __init__(self, sems, v, name, eng):
        self.sems = sems
        self.v = v
        self.name = name
        self.eng = eng

    def dinc(self, engname, s, n=1):
        if engname == self.name:
            self.eng.drain().then_inc(self.sems[s], n)
        self.v[s] += n
        return self.v[s]

    def minc(self, engname, inst_fn, s, n=1):
        if engname == self.name:
            inst_fn().then_inc(self.sems[s], n)
        self.v[s] += n
        return self.v[s]

    def dma(self, engname, s, out, in_, **kw):
        if engname == self.name:
            self.eng.dma_start(out=out, in_=in_, **kw).then_inc(self.sems[s], 16)
        self.v[s] += 16
        return self.v[s]

    def wait(self, engname, s, val):
        if engname == self.name and val > 0:
            self.eng.wait_ge(self.sems[s], val)

    def run(self, engname, fn):
        if engname == self.name:
            fn(self.eng)

    def tinc(self, engname, fn, s, n=1):
        if engname == self.name:
            inst = fn(self.eng)
            inst.then_inc(self.sems[s], n)
        self.v[s] += n
        return self.v[s]


def multi_emit(nc, blk, sems, V, gen):
    base = dict(V)
    final = None
    for name, sec in [("tensor", blk.tensor), ("scalar", blk.scalar),
                      ("vector", blk.vector), ("sync", blk.sync),
                      ("gpsimd", blk.gpsimd)]:
        gc = GC(sems, dict(base), name, None)

        def f(eng, gc=gc):
            gc.eng = eng
            gen(gc)
        sec(f)
        if final is None:
            final = gc.v
        else:
            assert final == gc.v, f"pass divergence: {name}"
    V.clear()
    V.update(final)


def build_program(phases=(1, 2, 3, 4, 5), debug=False):
    nc = bass.Bass("TRN2", target_bir_lowering=False)

    def dram(name, shape, dt, kind):
        return nc.dram_tensor(name, shape, dt, kind=kind).ap()

    IN, OUT_K, INT = "ExternalInput", "ExternalOutput", "Internal"
    dbg_kind = OUT_K if debug else INT

    x = dram("x", [NT, C], F32, IN)
    wq = dram("wq", [C, C], F32, IN)
    bq = dram("bq", [C], F32, IN)
    wkd = dram("wkd", [C, 2 * LAT], F32, IN)
    bkd = dram("bkd", [2 * LAT], F32, IN)
    wku = dram("wku", [LAT, 2 * C], F32, IN)
    bku = dram("bku", [2 * C], F32, IN)
    wo = dram("wo", [C, C], F32, IN)
    bo = dram("bo", [C], F32, IN)
    rw = dram("rw", [C, E], F32, IN)
    rsc = dram("rsc", [C], F32, IN)
    pes = dram("pes", [E], F32, IN)
    gw = dram("gw", [E, 2, HID, C], F32, IN)
    lw = dram("lw", [E, HID, C], F32, IN)
    cstb = dram("cstb", [128, 512], BF16, IN)
    cstf = dram("cstf", [128, 384], F32, IN)

    out = dram("out", [NT, C], F32, OUT_K)
    xn_park = dram("xn_park", [8, 128, NT], BF16, dbg_kind)
    ao_park = dram("ao_park", [8, 128, NT], BF16, dbg_kind)
    g0t_s = dram("g0t_s", [E, 16, 128, 1024], BF16, dbg_kind)
    g1t_s = dram("g1t_s", [E, 16, 128, 1024], BF16, dbg_kind)
    lw_s = dram("lw_s", [E, 16, 128, 1024], BF16, dbg_kind)
    sums_park = dram("sums_park", [SEQ, 8, 4, 2, 512], F32, dbg_kind)
    rcp_park = dram("rcp_park", [32, 512], BF16, dbg_kind)
    c_dbg = dram("c_dbg", [128, NTILE * E], F32, OUT_K) if debug else None
    lg_dbg = dram("lg_dbg", [128, NTILE * E], F32, OUT_K) if debug else None
    sp_dbg = dram("sp_dbg", [128, NTILE * 2], F32, OUT_K) if debug else None
    st_dbg = dram("st_dbg", [128, NTILE * 4], F32, OUT_K) if debug else None
    cball_dbg = dram("cball_dbg", [128, 8192], BF16, OUT_K) if debug else None
    aostg_dbg = dram("aostg_dbg", [128, 8192], BF16, OUT_K) if debug else None
    outinit_dbg = dram("outinit_dbg", [128, 8192], F32, OUT_K) if debug else None
    act_dbg = dram("act_dbg", [128, 16384], BF16, OUT_K) if debug else None
    oute0_dbg = dram("oute0_dbg", [128, 8192], F32, OUT_K) if debug else None
    gw_dbg = dram("gw_dbg", [128, 8192], BF16, OUT_K) if debug else None

    with contextlib.ExitStack() as top:
        ET = top.enter_context
        sems = {}
        for s in ["bar", "lx", "lw", "lg", "pk", "so",
                  "p1", "p2", "v1", "v2", "s1", "s2", "g1",
                  "wx0", "wx1", "wg0", "wg1", "wg2", "wg3", "wg4", "wg5",
                  "wg6", "wg7", "sg0", "sg1", "sg2", "sg3", "sg4", "sg5",
                  "sg6", "sg7", "wtb", "st0", "st1"]:
            sems[s] = ET(nc.semaphore(s))

        V = {}
        cb_sb = ET(nc.sbuf_tensor("cb_sb", [128, 512], BF16))
        cf_sb = ET(nc.sbuf_tensor("cf_sb", [128, 384], F32))
        wo_sb = ET(nc.sbuf_tensor("wo_sb", [128, 8 * C], BF16))
        bo_row = ET(nc.sbuf_tensor("bo_row", [1, C], BF16))
        c_sb = ET(nc.sbuf_tensor("c_sb", [128, NTILE * E], F32))
        c_bf = ET(nc.sbuf_tensor("c_bf", [128, NTILE * E], BF16))
        eps_sb = ET(nc.sbuf_tensor("eps_sb", [128, 4], F32))

        barv = [0]

        def barrier(p):
            engs = [p.blk.sync, p.blk.gpsimd, p.blk.tensor, p.blk.vector, p.blk.scalar]
            names = ["sync", "gpsimd", "tensor", "vector", "scalar"]
            barv[0] += 5
            tgt = barv[0]
            for sec in engs:
                def f(e):
                    e.drain().then_inc(sems["bar"], 1)
                    e.wait_ge(sems["bar"], tgt)
                sec(f)

        pw = contextlib.ExitStack()
        EW = pw.enter_context
        wq_sb = EW(nc.sbuf_tensor("wq_sb", [128, 8192], BF16))
        wkd_sb = EW(nc.sbuf_tensor("wkd_sb", [128, 4096], BF16))
        wku_sb = EW(nc.sbuf_tensor("wku_sb", [128, 4096], BF16))
        bq_sb = EW(nc.sbuf_tensor("bq_sb", [128, 8], F32))
        bkd_sb = EW(nc.sbuf_tensor("bkd_sb", [128, 4], F32))
        bkuk_sb = EW(nc.sbuf_tensor("bkuk_sb", [128, 8], F32))
        bkuv_row = EW(nc.sbuf_tensor("bkuv_row", [1, 1024], BF16))

        with contextlib.ExitStack() as ph:
            EP = ph.enter_context
            x_g = [EP(nc.sbuf_tensor(f"x_g{i}", [128, 4096], F32)) for i in range(2)]
            xn_g = EP(nc.sbuf_tensor("xn_g", [128, 4096], BF16))
            stage = [EP(nc.sbuf_tensor(f"stage{i}", [128, 4096], BF16)) for i in range(2)]
            sqdump = EP(nc.sbuf_tensor("sqdump", [128, 8 * C], F32))
            rw_b = EP(nc.sbuf_tensor("rw_b", [128, 64], BF16))
            rw_f = EP(nc.sbuf_tensor("rw_f", [128, 64], F32))
            rsc_f = EP(nc.sbuf_tensor("rsc_f", [128, 8], F32))
            pes_row = EP(nc.sbuf_tensor("pes_row", [1, 8], F32))
            pes_b = EP(nc.sbuf_tensor("pes_b", [128, 8], F32))
            sx_g = EP(nc.sbuf_tensor("sx_g", [128, 4], F32))
            ssq_g = EP(nc.sbuf_tensor("ssq_g", [128, 4], F32))
            m_g = EP(nc.sbuf_tensor("m_g", [128, 4], F32))
            ex2_g = EP(nc.sbuf_tensor("ex2_g", [128, 4], F32))
            nm2_g = EP(nc.sbuf_tensor("nm2_g", [128, 4], F32))
            var_g = EP(nc.sbuf_tensor("var_g", [128, 4], F32))
            sd_g = EP(nc.sbuf_tensor("sd_g", [128, 4], F32))
            rstd_g = EP(nc.sbuf_tensor("rstd_g", [128, 4], F32))
            t1_g = EP(nc.sbuf_tensor("t1_g", [128, 4], F32))
            vxn_g = EP(nc.sbuf_tensor("vxn_g", [128, 4], F32))
            sa_g = EP(nc.sbuf_tensor("sa_g", [128, 4], F32))
            rs_g = EP(nc.sbuf_tensor("rs_g", [128, 4], F32))
            sp_g = EP(nc.sbuf_tensor("sp_g", [128, 4], F32))
            lg8 = EP(nc.sbuf_tensor("lg8", [8, 512], F32))
            r_lg4 = EP(nc.sbuf_tensor("r_lg4", [128, 32], F32))
            r_sub = EP(nc.sbuf_tensor("r_sub", [128, 32], F32))
            r_ex4 = EP(nc.sbuf_tensor("r_ex4", [128, 32], F32))
            r_ge24 = EP(nc.sbuf_tensor("r_ge24", [128, 32], F32))
            r_nm1 = EP(nc.sbuf_tensor("r_nm1", [128, 1], F32))
            r_ge1 = EP(nc.sbuf_tensor("r_ge1", [128, 8], F32))
            r_msk = EP(nc.sbuf_tensor("r_msk", [128, 8], F32))
            r_nm2 = EP(nc.sbuf_tensor("r_nm2", [128, 1], F32))
            r_gt = EP(nc.sbuf_tensor("r_gt", [128, 8], F32))
            r_ss = EP(nc.sbuf_tensor("r_ss", [128, 1], F32))
            r_rc = EP(nc.sbuf_tensor("r_rc", [128, 1], F32))
            r_cf = EP(nc.sbuf_tensor("r_cf", [128, 8], F32))
            ps_x = [EP(nc.psum_tensor(f"ps_x{i}", [128, 1024], F32)) for i in range(2)]
            ps_lg = EP(nc.psum_tensor("ps_lg", [8, 512], F32))
            ps_lgt = EP(nc.psum_tensor("ps_lgt", [128, 32], F32))
            gbuf = [EP(nc.sbuf_tensor(f"gbuf{i}", [128, 1024], BF16)) for i in range(8)]
            tbuf = [EP(nc.sbuf_tensor(f"tbuf{i}", [128, 1024], BF16)) for i in range(8)]
            ps_t = [EP(nc.psum_tensor(f"ps_t{i}", [128, 512], F32)) for i in range(2)]
            blk = EP(nc.Block("p1"))
            p = Prog(nc, blk, sems, V or {k: 0 for k in sems})
            V.update(p.v); p.v = V
            ps_pes = ps_lgt[:, 0:8]

            def gen1(gc):
                gc.dma("gpsimd", "lg", out=cb_sb[:], in_=cstb)
                gc.dma("gpsimd", "lg", out=cf_sb[:], in_=cstf)
                gc.dma("gpsimd", "lg", out=rw_f[:], in_=rw.rearrange("(dc p) e -> p dc e", p=128))
                gc.dma("gpsimd", "lg", out=rsc_f[:], in_=rsc.rearrange("(dc p) -> p dc", p=128),
                       allow_slow_non_contiguous=True)
                gc.dma("gpsimd", "lg", out=pes_row[:], in_=pes.unsqueeze(0))
                gc.dma("gpsimd", "lg", out=wo_sb[:], in_=wo.rearrange("(dc p) c -> p dc c", p=128))
                gc.dma("gpsimd", "lg", out=bo_row[:], in_=bo.unsqueeze(0))
                LG_SMALL = gc.v["lg"]

                gc.run("vector", lambda v: (v.memset(eps_sb[:, 0:1], 1e-5),
                                            v.memset(eps_sb[:, 1:2], 1e-6)))
                gc.wait("vector", "lg", LG_SMALL)
                def rwprep(v):
                    for dc in range(8):
                        inst = v.tensor_scalar(out=rw_b[:, dc * 8:(dc + 1) * 8],
                                               in0=rw_f[:, dc * 8:(dc + 1) * 8],
                                               scalar1=rsc_f[:, dc:dc + 1], scalar2=None,
                                               op0=ALU.mult)
                    return inst
                gc.tinc("vector", rwprep, "v1", 1)

                gc.wait("tensor", "lg", LG_SMALL)
                gc.tinc("tensor", lambda t: t.matmul(out=ps_pes, lhsT=cf_sb[0:1, 256:384],
                                                     rhs=pes_row[:], start=True, stop=True), "p1", 1)
                gc.wait("vector", "p1", gc.v["p1"])
                gc.tinc("vector", lambda v: v.tensor_copy(out=pes_b[:], in_=ps_pes), "v1", 1)
                PESV = gc.v["v1"]

                xfree = {0: 0, 1: 0}
                stpark = {0: 0, 1: 0}
                evdone = {0: 0, 1: 0}
                for ts in range(8):
                    par = ts % 2
                    if xfree[par]:
                        gc.wait("sync", "v1", xfree[par])
                    sn = "wx0" if par == 0 else "wx1"
                    gc.dma("sync", sn, out=x_g[par][:],
                           in_=x[ts * 512:(ts + 1) * 512, :].rearrange("(tt p) c -> p tt c", p=128))
                    LXV = (sn, gc.v[sn])

                    gc.wait("scalar", LXV[0], LXV[1])
                    def sq4(s, par=par):
                        for tt4 in range(4):
                            inst = s.activation(out=sqdump[:, (par * 4 + tt4) * 1024:(par * 4 + tt4 + 1) * 1024],
                                                in_=x_g[par][:, tt4 * 1024:(tt4 + 1) * 1024],
                                                func=AF.Square, accum_out=ssq_g[:, tt4:tt4 + 1])
                        return inst
                    gc.tinc("scalar", sq4, "s1", 1)

                    gc.wait("vector", LXV[0], LXV[1])
                    gc.wait("vector", "s1", gc.v["s1"])
                    def stats(v, par=par):
                        for tt4 in range(4):
                            v.reduce_sum(out=sx_g[:, tt4:tt4 + 1],
                                         in_=x_g[par][:, tt4 * 1024:(tt4 + 1) * 1024], axis=AX.X)
                        v.drain()
                        v.tensor_scalar(out=m_g[:], in0=sx_g[:], scalar1=1.0 / C,
                                        scalar2=None, op0=ALU.mult)
                        v.tensor_scalar(out=ex2_g[:], in0=ssq_g[:], scalar1=1.0 / C,
                                        scalar2=None, op0=ALU.mult)
                        v.drain()
                        v.scalar_tensor_tensor(out=nm2_g[:], in0=m_g[:], scalar=-1.0,
                                               in1=m_g[:], op0=ALU.mult, op1=ALU.mult)
                        v.drain()
                        return v.tensor_tensor(out=var_g[:], in0=ex2_g[:], in1=nm2_g[:], op=ALU.add)
                    gc.tinc("vector", stats, "v1", 1)

                    gc.wait("scalar", "v1", gc.v["v1"])
                    gc.tinc("scalar", lambda s: s.activation(out=sd_g[:], in_=var_g[:], func=AF.Sqrt,
                                                             bias=eps_sb[:, 0:1]), "s1", 1)
                    gc.wait("vector", "s1", gc.v["s1"])
                    def mid(v):
                        v.reciprocal(out=rstd_g[:], in_=sd_g[:])
                        v.drain()
                        v.tensor_tensor(out=t1_g[:], in0=var_g[:], in1=rstd_g[:], op=ALU.mult)
                        v.drain()
                        return v.tensor_tensor(out=vxn_g[:], in0=t1_g[:], in1=rstd_g[:], op=ALU.mult)
                    gc.tinc("vector", mid, "v1", 1)

                    gc.wait("scalar", "v1", gc.v["v1"])
                    gc.tinc("scalar", lambda s: s.activation(out=sa_g[:], in_=vxn_g[:], func=AF.Sqrt,
                                                             bias=eps_sb[:, 1:2]), "s1", 1)
                    gc.wait("vector", "s1", gc.v["s1"])
                    def fin(v, par=par):
                        v.reciprocal(out=rs_g[:], in_=sa_g[:])
                        v.drain()
                        v.tensor_scalar(out=sp_g[:], in0=rs_g[:], scalar1=1.0 / 32.0,
                                        scalar2=None, op0=ALU.mult)
                        for tt4 in range(4):
                            inst = v.tensor_scalar(out=xn_g[:, tt4 * 1024:(tt4 + 1) * 1024],
                                                   in0=x_g[par][:, tt4 * 1024:(tt4 + 1) * 1024],
                                                   scalar1=m_g[:, tt4:tt4 + 1],
                                                   scalar2=rstd_g[:, tt4:tt4 + 1],
                                                   op0=ALU.subtract, op1=ALU.mult)
                        return inst
                    gc.tinc("vector", fin, "v1", 1)
                    xfree[par] = gc.v["v1"]
                    XNV = gc.v["v1"]

                    evs = []
                    for tt4 in range(4):
                        pp = tt4 % 2
                        gc.wait("tensor", "v1", XNV)
                        if evdone[pp]:
                            gc.wait("tensor", "v1", evdone[pp])
                        def tx(t, tt4=tt4, pp=pp):
                            for dc in range(8):
                                mm = t.matmul(out=ps_x[pp][:, dc * 128:(dc + 1) * 128],
                                              lhsT=xn_g[:, tt4 * 1024 + dc * 128: tt4 * 1024 + (dc + 1) * 128],
                                              rhs=cb_sb[:, 0:128], start=True, stop=True)
                            return mm
                        gc.tinc("tensor", tx, "p1", 1)
                        gc.wait("vector", "p1", gc.v["p1"])
                        if tt4 == 0 and stpark[par]:
                            gc.wait("vector", "st0" if par == 0 else "st1", stpark[par])
                        def ev(v, tt4=tt4, pp=pp, par=par):
                            return v.tensor_copy(
                                out=stage[par][:].rearrange("p (d t) -> p d t", d=8)[:, :, tt4 * 128:(tt4 + 1) * 128],
                                in_=ps_x[pp][:].rearrange("p (d t) -> p d t", d=8))
                        gc.tinc("vector", ev, "v1", 1)
                        evdone[pp] = gc.v["v1"]
                        evs.append(gc.v["v1"])

                    STGV = gc.v["v1"]
                    gc.wait("tensor", "v1", STGV)
                    def tlg(t, par=par):
                        for dc in range(8):
                            mm = t.matmul(out=ps_lg[:], lhsT=rw_b[:, dc * 8:(dc + 1) * 8],
                                          rhs=stage[par][:, dc * 512:(dc + 1) * 512],
                                          start=(dc == 0), stop=(dc == 7))
                        return mm
                    gc.tinc("tensor", tlg, "p1", 1)
                    gc.wait("vector", "p1", gc.v["p1"])
                    gc.tinc("vector", lambda v: v.tensor_copy(out=lg8[:], in_=ps_lg[:]), "v1", 1)
                    gc.wait("tensor", "v1", gc.v["v1"])
                    def tlgt(t):
                        for tt4 in range(4):
                            mm = t.matmul(out=ps_lgt[:, tt4 * 8:(tt4 + 1) * 8],
                                          lhsT=lg8[:, tt4 * 128:(tt4 + 1) * 128],
                                          rhs=cf_sb[0:8, 0:8], start=True, stop=True)
                        return mm
                    gc.tinc("tensor", tlgt, "p1", 1)
                    LGTP = gc.v["p1"]

                    gc.wait("vector", "p1", LGTP)
                    if ts > 0:
                        gc.wait("vector", "s1", gc.v["s1"])
                    def rpre(v, ts=ts):
                        for tt4 in range(4):
                            sl = slice(tt4 * 8, (tt4 + 1) * 8)
                            v.tensor_scalar(out=r_lg4[:, sl], in0=ps_lgt[:, sl],
                                            scalar1=sp_g[:, tt4:tt4 + 1], scalar2=None, op0=ALU.mult)
                            v.drain()
                            v.reduce_max(out=r_nm1[:], in_=r_lg4[:, sl], axis=AX.X, negate=True)
                            v.drain()
                            v.tensor_scalar(out=r_sub[:, sl], in0=r_lg4[:, sl], scalar1=r_nm1[:],
                                            scalar2=None, op0=ALU.add)
                            v.tensor_scalar(out=r_ge1[:], in0=r_lg4[:, sl], scalar1=r_nm1[:],
                                            scalar2=0.0, op0=ALU.add, op1=ALU.is_ge)
                            v.drain()
                            v.scalar_tensor_tensor(out=r_msk[:], in0=r_ge1[:], scalar=BIGNEG,
                                                   in1=r_lg4[:, sl], op0=ALU.mult, op1=ALU.add)
                            v.drain()
                            v.reduce_max(out=r_nm2[:], in_=r_msk[:], axis=AX.X, negate=True)
                            v.drain()
                            inst = v.tensor_scalar(out=r_ge24[:, sl], in0=r_lg4[:, sl], scalar1=r_nm2[:],
                                                   scalar2=0.0, op0=ALU.add, op1=ALU.is_ge)
                        return inst
                    gc.tinc("vector", rpre, "v2", 1)
                    gc.wait("scalar", "v2", gc.v["v2"])
                    gc.tinc("scalar", lambda s: s.activation(out=r_ex4[:], in_=r_sub[:], func=AF.Exp),
                            "s1", 1)
                    gc.wait("vector", "s1", gc.v["s1"])
                    def rpost(v, ts=ts):
                        for tt4 in range(4):
                            sl = slice(tt4 * 8, (tt4 + 1) * 8)
                            tcol = (ts * 4 + tt4) * 8
                            v.tensor_tensor(out=r_gt[:], in0=r_ex4[:, sl], in1=r_ge24[:, sl], op=ALU.mult)
                            v.drain()
                            v.reduce_sum(out=r_ss[:], in_=r_gt[:], axis=AX.X)
                            v.drain()
                            v.reciprocal(out=r_rc[:], in_=r_ss[:])
                            v.drain()
                            v.tensor_scalar(out=r_cf[:], in0=r_gt[:], scalar1=r_rc[:],
                                            scalar2=None, op0=ALU.mult)
                            v.drain()
                            v.tensor_tensor(out=c_sb[:, tcol:tcol + 8], in0=r_cf[:],
                                            in1=pes_b[:], op=ALU.mult)
                            v.drain()
                            inst = v.tensor_copy(out=c_bf[:, tcol:tcol + 8], in_=c_sb[:, tcol:tcol + 8])
                        return inst
                    gc.tinc("vector", rpost, "v2", 1)

                    gc.wait("sync", "v1", STGV)
                    psn = "st0" if par == 0 else "st1"
                    gc.dma("sync", psn, out=xn_park[:, :, ts * 512:(ts + 1) * 512].rearrange("d p t -> p d t"),
                           in_=stage[par][:])
                    stpark[par] = gc.v[psn]

                if debug:
                    gc.wait("sync", "v2", gc.v["v2"])
                    gc.dma("sync", "so", out=c_dbg, in_=c_sb[:])
                gc.wait("sync", "st0", gc.v["st0"])
                gc.wait("sync", "st1", gc.v["st1"])
                if gc.v["so"]:
                    gc.wait("sync", "so", gc.v["so"])

            multi_emit(nc, blk, sems, V, gen1)

            if 2 in phases:
                WG_BASE = {i: p.v[f"wg{i}"] for i in range(8)}
                SG_BASE = {i: p.v[f"sg{i}"] for i in range(8)}
                P2_0 = p.v["p1"]
                V2_0 = p.v["v1"]
                S2_0 = p.v["s1"]

                def g_attnw(g):
                    p.inc(g.dma_start(out=wq_sb[:], in_=wq.rearrange("(dc p) c -> p dc c", p=128)), "lw", 16)
                    p.inc(g.dma_start(out=wkd_sb[:], in_=wkd.rearrange("(dc p) f -> p dc f", p=128)), "lw", 16)
                    p.inc(g.dma_start(out=wku_sb[:], in_=wku.rearrange("(lc p) f -> p lc f", p=128)), "lw", 16)
                    p.inc(g.dma_start(out=bq_sb[:], in_=bq.rearrange("(cc p) -> p cc", p=128),
                                      allow_slow_non_contiguous=True), "lw", 16)
                    p.inc(g.dma_start(out=bkd_sb[:], in_=bkd.rearrange("(lc p) -> p lc", p=128),
                                      allow_slow_non_contiguous=True), "lw", 16)
                    p.inc(g.dma_start(out=bkuk_sb[:], in_=bku[0:1024].rearrange("(cc p) -> p cc", p=128),
                                      allow_slow_non_contiguous=True), "lw", 16)
                    p.inc(g.dma_start(out=bkuv_row[:], in_=bku[1024:2048].unsqueeze(0)), "lw", 16)
                blk.gpsimd(g_attnw)

                PB = [ps_t[0][:], ps_t[1][:],
                      ps_x[0][:, 0:512], ps_x[0][:, 512:1024],
                      ps_x[1][:, 0:512], ps_x[1][:, 512:1024]]

                NU = E * 2 * 16
                for u in range(NU):
                    e, gx, hc = u // 32, (u // 16) % 2, u % 16
                    par = u % 8
                    gsrc = gw[e, gx, hc * 128:(hc + 1) * 128, :]
                    gdst = (g0t_s if gx == 0 else g1t_s)[e, hc]

                    def f_g(g, u=u, par=par, gsrc=gsrc):
                        if u >= 8:
                            g.wait_ge(sems["p1"], P2_0 + 2 * (u - 8) + 2)
                        p.inc(g.dma_start(out=gbuf[par][:], in_=gsrc), f"wg{par}", 16)
                    blk.gpsimd(f_g)

                    def f_t(t, u=u, par=par):
                        t.wait_ge(sems[f"wg{par}"], WG_BASE[par] + 16 * (u // 8 + 1))
                        for hf in range(2):
                            uh = 2 * u + hf
                            if uh >= 6:
                                if uh % 2 == 0:
                                    t.wait_ge(sems["v1"], V2_0 + (uh - 6) // 2 + 1)
                                else:
                                    t.wait_ge(sems["s1"], S2_0 + (uh - 5) // 2 + 1)
                            elif uh % 6 >= 2:
                                t.wait_ge(sems["v1"], V2_0)
                            mm = None
                            for dc in range(4):
                                mm = t.matmul(out=PB[uh % 6][:, dc * 128:(dc + 1) * 128],
                                              lhsT=gbuf[par][:, hf * 512 + dc * 128: hf * 512 + (dc + 1) * 128],
                                              rhs=cb_sb[:, 0:128], start=True, stop=True)
                            p.inc(mm, "p1", 1)
                    blk.tensor(f_t)

                    def f_v(v, u=u, par=par):
                        v.wait_ge(sems["p1"], P2_0 + 2 * u + 1)
                        if u >= 8:
                            v.wait_ge(sems[f"sg{par}"], SG_BASE[par] + 16 * (u // 8))
                        inst = v.tensor_copy(out=tbuf[par][:, 0:512], in_=PB[(2 * u) % 6])
                        p.inc(inst, "v1", 1)
                    blk.vector(f_v)

                    def f_sc(s, u=u, par=par):
                        s.wait_ge(sems["p1"], P2_0 + 2 * u + 2)
                        if u >= 8:
                            s.wait_ge(sems[f"sg{par}"], SG_BASE[par] + 16 * (u // 8))
                        inst = s.activation(out=tbuf[par][:, 512:1024], in_=PB[(2 * u + 1) % 6],
                                            func=AF.Identity)
                        p.inc(inst, "s1", 1)
                    blk.scalar(f_sc)

                    def f_s(s, u=u, par=par, gdst=gdst):
                        s.wait_ge(sems["v1"], V2_0 + u + 1)
                        s.wait_ge(sems["s1"], S2_0 + u + 1)
                        p.inc(s.dma_start(out=gdst, in_=tbuf[par][:]), f"sg{par}", 16)
                    blk.sync(f_s)

                def f_end2(s):
                    for i in range(8):
                        s.wait_ge(sems[f"sg{i}"], p.v[f"sg{i}"])
                blk.sync(f_end2)

                def f_end2g(g):
                    if p.v["pk"]:
                        g.wait_ge(sems["pk"], p.v["pk"])
                    g.wait_ge(sems["lw"], p.v["lw"])
                blk.gpsimd(f_end2g)
            barrier(p)

        if 3 in phases:
            with contextlib.ExitStack() as ph:
                EP = ph.enter_context
                xts = [EP(nc.sbuf_tensor(f"xts{i}", [128, 4096], BF16)) for i in range(2)]
                qf = EP(nc.sbuf_tensor("qf", [128, 16384], BF16))
                kf = EP(nc.sbuf_tensor("kf", [128, 16384], BF16))
                va = EP(nc.sbuf_tensor("va", [128, 16896], BF16))
                latf = EP(nc.sbuf_tensor("latf", [128, 4096], BF16))
                kvl = EP(nc.sbuf_tensor("kvl", [128, 2048], F32))
                expT = [EP(nc.sbuf_tensor(f"expT{i}", [128, 512], BF16)) for i in range(6)]
                qm2 = EP(nc.sbuf_tensor("qm2", [128, 8], F32))
                km2 = EP(nc.sbuf_tensor("km2", [128, 8], F32))
                nrm_t = EP(nc.sbuf_tensor("nrm_t", [128, 1], F32))
                n_a = EP(nc.sbuf_tensor("n_a", [1, 1], F32))
                n_b = EP(nc.sbuf_tensor("n_b", [1, 1], F32))
                n_c = EP(nc.sbuf_tensor("n_c", [1, 1], F32))
                n_bf = EP(nc.sbuf_tensor("n_bf", [1, 1], BF16))
                sbias = EP(nc.sbuf_tensor("sbias", [128, 16], F32))
                rr_sb = EP(nc.sbuf_tensor("rr_sb", [128, 1024], F32))
                rr_bf = EP(nc.sbuf_tensor("rr_bf", [128, 1024], BF16))
                rb_sb = EP(nc.sbuf_tensor("rb_sb", [64, 1024], F32))
                ao_sb = [EP(nc.sbuf_tensor(f"ao_sb{i}", [64, 512], BF16)) for i in range(2)]
                ps_sc = [EP(nc.psum_tensor(f"ps_sc{i}", [128, 512], F32)) for i in range(6)]
                ps_oa = [EP(nc.psum_tensor(f"ps_oa{i}", [65, 512], F32)) for i in range(2)]
                ps_pj = [ps_sc[0], ps_sc[1]]
                ps_n1 = ps_sc[0][0:1, 0:1]
                ps_n2 = ps_sc[0][0:1, 4:5]
                ps_sbias = ps_sc[1][:, 0:1]
                ps_rb = ps_sc[4][0:64, 0:512]
                ps_rb2 = ps_sc[5][0:64, 0:512]
                blk = EP(nc.Block("p3"))
                p = Prog(nc, blk, sems, V)

                def gen3(gc):
                    for s_i in range(SEQ):
                        soff = s_i * T
                        PJ0 = gc.v["p1"]
                        SE0 = gc.v["s1"]
                        LX0 = gc.v["lx"]

                        gc.wait("vector", "v2", gc.v["v2"])
                        gc.run("vector", lambda v: v.memset(
                            va[:].rearrange("p (kb h o) -> p kb h o", kb=16, h=16)[:, :, :, 64:65], 1.0))
                        gc.dinc("vector", "v1", 1)
                        VA1 = gc.v["v1"]

                        pj = [0]
                        pj_at_ts = {}
                        lxv = {}

                        def proj_group(mm_fn, evac_eng, evac_fn, extra_wait=None):
                            g = pj[0]
                            par = g % 2
                            if g == 0 and s_i > 0:
                                gc.wait("tensor", "v2", gc.v["v2"])
                            if extra_wait is not None:
                                gc.wait("tensor", extra_wait[0], extra_wait[1])
                            if g >= 2:
                                gc.wait("tensor", "s1", SE0 + g - 1)
                            gc.tinc("tensor", lambda t: mm_fn(t, par), "p1", 1)
                            gc.wait(evac_eng, "p1", PJ0 + g + 1)
                            gc.tinc(evac_eng, lambda e: evac_fn(e, par), "s1", 1)
                            pj[0] += 1

                        for ts in range(4):
                            if ts >= 2:
                                gc.wait("sync", "p1", PJ0 + pj_at_ts[ts - 2])
                            sn = "wx0" if ts % 2 == 0 else "wx1"
                            gc.dma("sync", sn, out=xts[ts % 2][:],
                                   in_=xn_park[:, :, soff + ts * 512: soff + (ts + 1) * 512].rearrange("d p t -> p d t"))
                            lxv[ts] = (sn, gc.v[sn])

                            for cc in range(8):
                                def mm_q(t, par, ts=ts, cc=cc):
                                    t.wait_ge(sems[lxv[ts][0]], lxv[ts][1])
                                    for dc in range(8):
                                        mm = t.matmul(out=ps_pj[par][:],
                                                 lhsT=wq_sb[:, dc * 1024 + cc * 128: dc * 1024 + (cc + 1) * 128],
                                                 rhs=xts[ts % 2][:, dc * 512:(dc + 1) * 512],
                                                 start=(dc == 0), stop=(dc == 7))
                                    return mm

                                def ev_q(s, par, ts=ts, cc=cc):
                                    return s.activation(out=qf[:, cc * 2048 + ts * 512: cc * 2048 + (ts + 1) * 512],
                                                        in_=ps_pj[par][:], func=AF.Identity,
                                                        bias=bq_sb[:, cc:cc + 1])
                                proj_group(mm_q, "scalar", ev_q)

                            for lc4 in range(4):
                                def mm_kv(t, par, ts=ts, lc4=lc4):
                                    t.wait_ge(sems[lxv[ts][0]], lxv[ts][1])
                                    for dc in range(8):
                                        mm = t.matmul(out=ps_pj[par][:],
                                                 lhsT=wkd_sb[:, dc * 512 + lc4 * 128: dc * 512 + (lc4 + 1) * 128],
                                                 rhs=xts[ts % 2][:, dc * 512:(dc + 1) * 512],
                                                 start=(dc == 0), stop=(dc == 7))
                                    return mm

                                def ev_kv(s, par, ts=ts, lc4=lc4, VA1=VA1):
                                    if lc4 == 0:
                                        s.wait_ge(sems["v1"], VA1 + 2 * ts)
                                    return s.activation(out=kvl[:, lc4 * 512:(lc4 + 1) * 512],
                                                        in_=ps_pj[par][:], func=AF.Gelu,
                                                        bias=bkd_sb[:, lc4:lc4 + 1])
                                proj_group(mm_kv, "scalar", ev_kv)

                            gc.wait("vector", "s1", SE0 + pj[0])
                            def lat_add(v, ts=ts):
                                for lc in range(2):
                                    inst = v.tensor_tensor(out=latf[:, lc * 2048 + ts * 512: lc * 2048 + (ts + 1) * 512],
                                                           in0=kvl[:, lc * 512:(lc + 1) * 512],
                                                           in1=kvl[:, (2 + lc) * 512:(3 + lc) * 512], op=ALU.add)
                                return inst
                            gc.tinc("vector", lat_add, "v1", 2)
                            LATV = gc.v["v1"]

                            for cc in range(8):
                                def mm_k(t, par, ts=ts, cc=cc):
                                    for lc in range(2):
                                        mm = t.matmul(out=ps_pj[par][:],
                                                 lhsT=wku_sb[:, lc * 2048 + cc * 128: lc * 2048 + (cc + 1) * 128],
                                                 rhs=latf[:, lc * 2048 + ts * 512: lc * 2048 + (ts + 1) * 512],
                                                 start=(lc == 0), stop=(lc == 1))
                                    return mm

                                def ev_k(s, par, ts=ts, cc=cc):
                                    return s.activation(out=kf[:, cc * 2048 + ts * 512: cc * 2048 + (ts + 1) * 512],
                                                        in_=ps_pj[par][:], func=AF.Identity,
                                                        bias=bkuk_sb[:, cc:cc + 1])
                                proj_group(mm_k, "scalar", ev_k, extra_wait=("v1", LATV))

                            for tt4 in range(4):
                                for dh in range(2):
                                    def mm_v(t, par, ts=ts, tt4=tt4, dh=dh):
                                        for lc in range(2):
                                            t.matmul(out=ps_pj[par][:],
                                                     lhsT=latf[:, lc * 2048 + ts * 512 + tt4 * 128: lc * 2048 + ts * 512 + (tt4 + 1) * 128],
                                                     rhs=wku_sb[:, lc * 2048 + 1024 + dh * 512: lc * 2048 + 1024 + (dh + 1) * 512],
                                                     start=(lc == 0), stop=False)
                                        mm = t.matmul(out=ps_pj[par][:],
                                                      lhsT=cb_sb[0:1, 128:256],
                                                      rhs=bkuv_row[0:1, dh * 512:(dh + 1) * 512],
                                                      start=False, stop=True)
                                        return mm

                                    def ev_v(v, par, ts=ts, tt4=tt4, dh=dh):
                                        kb = ts * 4 + tt4
                                        return v.tensor_copy(
                                            out=va[:].rearrange("p (kb h o) -> p kb h o", kb=16, h=16)[:, kb, dh * 8:(dh + 1) * 8, 0:64],
                                            in_=ps_pj[par][:].rearrange("p (h d) -> p h d", h=8))
                                    proj_group(mm_v, "vector", ev_v, extra_wait=("v1", LATV))

                            pj_at_ts[ts] = pj[0]

                        gc.wait("vector", "s1", SE0 + pj[0])
                        def vnorm(v):
                            for cc in range(8):
                                v.reduce_max(out=nrm_t[:], in_=qf[:, cc * 2048:(cc + 1) * 2048],
                                             axis=AX.X, apply_absolute_value=True)
                                v.drain()
                                v.tensor_scalar(out=qm2[:, cc:cc + 1], in0=nrm_t[:],
                                                scalar1=nrm_t[:], scalar2=None, op0=ALU.mult)
                                v.reduce_max(out=nrm_t[:], in_=kf[:, cc * 2048:(cc + 1) * 2048],
                                             axis=AX.X, apply_absolute_value=True)
                                v.drain()
                                v.tensor_scalar(out=km2[:, cc:cc + 1], in0=nrm_t[:],
                                                scalar1=nrm_t[:], scalar2=None, op0=ALU.mult)
                        gc.run("vector", vnorm)
                        gc.dinc("vector", "v1", 1)
                        NRMV = gc.v["v1"]

                        for h in range(16):
                            cc, odd = h // 2, h % 2
                            base = 64 * odd

                            gc.wait("tensor", "v1", NRMV + 3 * h)
                            def tn(t, cc=cc, base=base):
                                tp_ = (base, 0)
                                t.matmul(out=ps_n1, lhsT=cf_sb[base:base + 64, 256:257],
                                         rhs=qm2[base:base + 64, cc:cc + 1], start=True, stop=True,
                                         tile_position=tp_)
                                t.matmul(out=ps_n2, lhsT=cf_sb[base:base + 64, 256:257],
                                         rhs=km2[base:base + 64, cc:cc + 1], start=True, stop=True,
                                         tile_position=tp_)
                            gc.run("tensor", tn)
                            gc.dinc("tensor", "p2", 1)

                            gc.wait("vector", "p2", gc.v["p2"])
                            def vn1(v):
                                v.tensor_copy(out=n_a[:], in_=ps_n1)
                                v.drain()
                                v.tensor_scalar(out=n_b[:], in0=ps_n2, scalar1=n_a[:],
                                                scalar2=None, op0=ALU.mult)
                            gc.run("vector", vn1)
                            gc.dinc("vector", "v1", 1)

                            gc.wait("scalar", "v1", gc.v["v1"])
                            gc.run("scalar", lambda s: s.activation(out=n_c[:], in_=n_b[:], func=AF.Sqrt))
                            gc.dinc("scalar", "s1", 1)

                            gc.wait("vector", "s1", gc.v["s1"])
                            gc.run("vector", lambda v: v.tensor_scalar(
                                out=n_bf[:], in0=n_c[:], scalar1=-0.125, scalar2=None, op0=ALU.mult))
                            gc.dinc("vector", "v1", 1)

                            gc.wait("tensor", "v1", gc.v["v1"])
                            gc.run("tensor", lambda t: t.matmul(
                                out=ps_sbias, lhsT=cb_sb[0:1, 128:256],
                                rhs=n_bf[:], start=True, stop=True))
                            gc.dinc("tensor", "p2", 1)

                            gc.wait("vector", "p2", gc.v["p2"])
                            gc.run("vector", lambda v, h=h: v.tensor_copy(out=sbias[:, h:h + 1], in_=ps_sbias))
                            gc.dinc("vector", "v1", 1)

                        SBV = gc.v["v1"]

                        pk_last = [0]
                        for hp in range(8):
                            for qc in range(4):
                                nkb = 4 * qc + 4
                                QK0 = gc.v["p2"]
                                MV0 = gc.v["v2"]
                                EX0 = gc.v["s2"]
                                mask_v2 = {}

                                qk_p2 = {}
                                av_p2 = {}

                                def emit_qk(kb):
                                    par = kb % 3
                                    if hp == 0 and qc == 0 and kb == 0:
                                        gc.wait("tensor", "v1", SBV)
                                    if kb < 3:
                                        gc.wait("tensor", "s2", EX0)
                                    else:
                                        gc.wait("tensor", "s2", EX0 + (kb - 2))
                                    if kb == 2:
                                        gc.wait("tensor", "v2", MV0)
                                    def qk(t, hp=hp, qc=qc, kb=kb, par=par):
                                        for odd in range(2):
                                            bb = 64 * odd
                                            mm = t.matmul(out=ps_sc[2 * par + odd][:],
                                                          lhsT=kf[bb:bb + 64, hp * 2048 + kb * 128: hp * 2048 + (kb + 1) * 128],
                                                          rhs=qf[bb:bb + 64, hp * 2048 + qc * 512: hp * 2048 + (qc + 1) * 512],
                                                          start=True, stop=True, tile_position=(bb, 0))
                                        return mm
                                    gc.tinc("tensor", qk, "p2", 1)
                                    qk_p2[kb] = gc.v["p2"]

                                def emit_exp(kb):
                                    par = kb % 3
                                    lead = kb * 128 - qc * 512
                                    gc.wait("scalar", "p2", qk_p2[kb])
                                    if kb >= 3:
                                        gc.wait("scalar", "p2", av_p2[kb - 3])
                                        if (kb - 3) in mask_v2:
                                            gc.wait("scalar", "v2", mask_v2[kb - 3])
                                    def ex(s, hp=hp, kb=kb, par=par):
                                        for odd in range(2):
                                            inst = s.activation(out=expT[2 * par + odd][:],
                                                                in_=ps_sc[2 * par + odd][:], func=AF.Exp,
                                                                scale=0.125, bias=sbias[:, 2 * hp + odd:2 * hp + odd + 1])
                                        return inst
                                    gc.tinc("scalar", ex, "s2", 1)
                                    if lead >= 0:
                                        gc.wait("vector", "s2", EX0 + kb + 1)
                                        def vm(v, kb=kb, par=par, lead=lead):
                                            for odd in range(2):
                                                et = expT[2 * par + odd]
                                                if lead > 0:
                                                    v.memset(et[:, 0:lead], 0.0)
                                                inst = v.tensor_tensor(out=et[:, lead:lead + 128],
                                                                       in0=et[:, lead:lead + 128],
                                                                       in1=cb_sb[:, 256:384], op=ALU.mult)
                                            return inst
                                        gc.tinc("vector", vm, "v2", 1)
                                        mask_v2[kb] = gc.v["v2"]

                                def emit_av(kb):
                                    par = kb % 3
                                    gc.wait("tensor", "s2", EX0 + kb + 1)
                                    if kb in mask_v2:
                                        gc.wait("tensor", "v2", mask_v2[kb])
                                    if kb == 0:
                                        gc.wait("tensor", "v2", MV0)
                                    def av(t, hp=hp, kb=kb, par=par, nkb=nkb):
                                        for odd in range(2):
                                            h = 2 * hp + odd
                                            mm = t.matmul(out=ps_oa[odd][:],
                                                          lhsT=va[:, kb * 1056 + h * 66: kb * 1056 + h * 66 + 65],
                                                          rhs=expT[2 * par + odd][:],
                                                          start=(kb == 0), stop=(kb == nkb - 1))
                                        return mm
                                    gc.tinc("tensor", av, "p2", 1)
                                    av_p2[kb] = gc.v["p2"]

                                for kb in range(nkb):
                                    emit_qk(kb)
                                    emit_exp(kb)
                                    if kb >= 2:
                                        emit_av(kb - 2)
                                emit_av(nkb - 2)
                                emit_av(nkb - 1)

                                AV_END = gc.v["p2"]
                                gc.wait("vector", "p2", AV_END)
                                if pk_last[0] > 0:
                                    gc.wait("vector", "pk", pk_last[0])
                                def vao(v):
                                    v.tensor_copy(out=rr_sb[64:65, 0:512], in_=ps_oa[0][64:65, :])
                                    v.tensor_copy(out=rr_sb[64:65, 512:1024], in_=ps_oa[1][64:65, :])
                                    v.tensor_copy(out=ao_sb[0][:], in_=ps_oa[0][0:64, :])
                                    return v.tensor_copy(out=ao_sb[1][:], in_=ps_oa[1][0:64, :])
                                gc.tinc("vector", vao, "v2", 1)

                                gc.wait("gpsimd", "v2", gc.v["v2"])
                                for odd in range(2):
                                    gc.dma("gpsimd", "pk",
                                           out=ao_park[hp, odd * 64:(odd + 1) * 64,
                                                       s_i * T + qc * 512:s_i * T + (qc + 1) * 512],
                                           in_=ao_sb[odd][:])
                                gc.dma("gpsimd", "pk",
                                       out=sums_park[s_i, hp, qc].rearrange("o t -> (o t)").unsqueeze(0),
                                       in_=rr_sb[64:65, :])
                                pk_last[0] = gc.v["pk"]

                multi_emit(nc, blk, sems, V, gen3)

                def f_end3(g):
                    g.wait_ge(sems["pk"], p.v["pk"])
                blk.gpsimd(f_end3)
                barrier(p)
        pw.close()

        if 5 in phases:
            with contextlib.ExitStack() as ph:
                EP = ph.enter_context
                xn_tb = EP(nc.sbuf_tensor("xn_tb", [128, 8192], BF16))
                x_tt = [EP(nc.sbuf_tensor(f"x_tt{i}", [128, 1024], F32)) for i in range(2)]
                out_sb = EP(nc.sbuf_tensor("out_sb", [128, 8192], F32))
                act = EP(nc.sbuf_tensor("act", [128, 16384], BF16))
                gel = [EP(nc.sbuf_tensor(f"gel{i}", [128, 512], BF16)) for i in range(2)]
                tmp_b = [EP(nc.sbuf_tensor(f"tmp_b{i}", [128, 512], BF16)) for i in range(2)]
                g0b = [EP(nc.sbuf_tensor(f"g0b{i}", [128, 4096], BF16)) for i in range(2)]
                g1b = [EP(nc.sbuf_tensor(f"g1b{i}", [128, 4096], BF16)) for i in range(2)]
                lw_e = EP(nc.sbuf_tensor("lw_e", [128, 16384], BF16))
                cball = EP(nc.sbuf_tensor("cball", [128, 8192], BF16))
                ctrow = EP(nc.sbuf_tensor("ctrow", [1, 1024], BF16))
                sums2 = EP(nc.sbuf_tensor("sums2", [32, 512], F32))
                rcp_b = EP(nc.sbuf_tensor("rcp_b", [32, 512], BF16))
                rcp2 = EP(nc.sbuf_tensor("rcp2", [2, 4096], BF16))
                ps_h0 = [EP(nc.psum_tensor(f"ps_h0{i}", [128, 512], F32)) for i in range(2)]
                ps_h1 = [EP(nc.psum_tensor(f"ps_h1{i}", [128, 512], F32)) for i in range(2)]
                ps_o = [EP(nc.psum_tensor(f"ps_o{i}", [128, 1024], F32)) for i in range(2)]
                blk = EP(nc.Block("p5"))
                p = Prog(nc, blk, sems, V)

                def gen5(gc):
                    SO_STORE = [0]
                    gelu_done = {0: 0, 1: 0}
                    act_done = {0: 0, 1: 0}
                    outev_done = {0: 0, 1: 0}
                    gstr_done = {0: 0, 1: 0}
                    RCPV = [0]
                    PKV = [0]
                    CHP = [0]
                    ST0P = [0]

                    for tb in range(NTB):
                        if tb >= 1:
                            gc.wait("sync", "p1", gc.v["p1"])
                            gc.wait("sync", "p2", gc.v["p2"])
                            gc.wait("sync", "v1", RCPV[0])
                        gc.dma("sync", "wtb", out=xn_tb[:],
                               in_=xn_park[:, :, tb * TB:(tb + 1) * TB].rearrange("d p t -> p d t"))
                        gc.dma("sync", "wtb", out=act[:, 0:8192],
                               in_=ao_park[:, :, tb * TB:(tb + 1) * TB].rearrange("d p t -> p d t"))
                        s_i5, q01 = tb // 2, (tb % 2) * 2
                        for q2 in range(2):
                            for o2 in range(2):
                                r0 = q2 * 16 + o2 * 8
                                gc.dma("sync", "wtb", out=sums2[r0:r0 + 8],
                                       in_=sums_park[s_i5, :, q01 + q2, o2])
                        TBLW = gc.v["wtb"]

                        gc.wait("vector", "wtb", TBLW)
                        if PKV[0]:
                            gc.wait("vector", "pk", PKV[0])
                        def vrcp(v):
                            v.reciprocal(out=sums2[:], in_=sums2[:])
                            v.drain()
                            return v.tensor_copy(out=rcp_b[:], in_=sums2[:])
                        gc.tinc("vector", vrcp, "v1", 1)
                        RCPV[0] = gc.v["v1"]
                        gc.wait("gpsimd", "v1", RCPV[0])
                        if ST0P[0]:
                            gc.wait("gpsimd", "st0", ST0P[0])
                        gc.dma("gpsimd", "pk", out=rcp_park, in_=rcp_b[:])
                        PKV[0] = gc.v["pk"]

                        for e in range(E):
                            if e == 0:
                                if tb >= 1:
                                    gc.wait("tensor", "v1", gc.v["v1"])
                                    gc.wait("tensor", "s2", gc.v["s2"])
                                    gc.wait("tensor", "v2", gc.v["v2"])
                            else:
                                gc.wait("tensor", "v1", gc.v["v1"])
                            def tct(t, tb=tb, e=e):
                                for tt in range(8):
                                    mm = t.matmul(out=ps_h0[tt // 4][0:1, (tt % 4) * 128:(tt % 4 + 1) * 128],
                                                  lhsT=c_bf[:, (tb * 8 + tt) * 8 + e:(tb * 8 + tt) * 8 + e + 1],
                                                  rhs=cb_sb[:, 0:128], start=True, stop=True)
                                return mm
                            gc.tinc("tensor", tct, "p1", 1)

                            gc.wait("vector", "p1", gc.v["p1"])
                            def vct(v):
                                v.tensor_copy(out=ctrow[0:1, 0:512], in_=ps_h0[0][0:1, :])
                                v.tensor_copy(out=ctrow[0:1, 512:1024], in_=ps_h0[1][0:1, :])
                            gc.run("vector", vct)
                            gc.dinc("vector", "v1", 1)

                            gc.wait("tensor", "v1", gc.v["v1"])
                            def tcb(t):
                                for nh in range(2):
                                    mm = t.matmul(out=ps_h1[nh][:], lhsT=cb_sb[0:1, 128:256],
                                                  rhs=ctrow[0:1, nh * 512:(nh + 1) * 512],
                                                  start=True, stop=True)
                                return mm
                            gc.tinc("tensor", tcb, "p1", 1)

                            gc.wait("vector", "p1", gc.v["p1"])
                            def vcb(v, e=e):
                                for nh in range(2):
                                    inst = v.tensor_copy(out=cball[:, e * 1024 + nh * 512: e * 1024 + (nh + 1) * 512],
                                                         in_=ps_h1[nh][:])
                                return inst
                            gc.tinc("vector", vcb, "v1", 1)

                        CBV = gc.v["v1"]

                        SCB = gc.v["v1"]
                        for ch in range(16):
                            half, chh = ch // 8, ch % 8
                            if chh == 0:
                                gc.wait("gpsimd", "pk", PKV[0])
                                if CHP[0]:
                                    gc.wait("gpsimd", "p2", CHP[0])
                                for q2 in range(2):
                                    gc.dma("gpsimd", "st0",
                                           out=rcp2[:].rearrange("o (hp x) -> o hp x", hp=4)
                                           [:, :, q2 * 512:(q2 + 1) * 512],
                                           in_=rcp_park[q2 * 16:q2 * 16 + 16]
                                           .rearrange("(o hp) t -> o hp t", o=2)
                                           [:, 4 * half:4 * (half + 1), :])
                                gc.wait("tensor", "st0", gc.v["st0"])
                                gc.wait("tensor", "wtb", TBLW)
                            if ch >= 2:
                                gc.wait("tensor", "v1", SCB + ch - 1)
                            else:
                                gc.wait("tensor", "v1", CBV)
                            def tbb(t, ch=ch, chh=chh):
                                return t.matmul(out=ps_h0[ch % 2][:],
                                                lhsT=cb_sb[0:2, 384:512],
                                                rhs=rcp2[0:2, chh * 512:(chh + 1) * 512],
                                                start=True, stop=True)
                            gc.tinc("tensor", tbb, "p2", 1)
                            CHP[0] = gc.v["p2"]
                            gc.wait("vector", "p2", gc.v["p2"])
                            def vsc(v, ch=ch):
                                return v.tensor_tensor(out=act[:, ch * 512:(ch + 1) * 512],
                                                       in0=act[:, ch * 512:(ch + 1) * 512],
                                                       in1=ps_h0[ch % 2][:], op=ALU.mult)
                            gc.tinc("vector", vsc, "v1", 1)
                        SCALED = gc.v["v1"]
                        ST0P[0] = gc.v["st0"]

                        lxv5 = {}
                        for tt in range(8):
                            if outev_done[tt % 2]:
                                gc.wait("sync", "v1", outev_done[tt % 2])
                            sn = "wx0" if tt % 2 == 0 else "wx1"
                            gc.dma("sync", sn, out=x_tt[tt % 2][:],
                                   in_=x[tb * TB + tt * 128: tb * TB + (tt + 1) * 128, :])
                            lxv5[tt] = (sn, gc.v[sn])

                            if tt == 0:
                                gc.wait("tensor", "v1", SCALED)
                                gc.wait("tensor", "wtb", TBLW)
                                if SO_STORE[0]:
                                    gc.wait("tensor", "so", SO_STORE[0])
                            gc.wait("tensor", lxv5[tt][0], lxv5[tt][1])
                            if outev_done[tt % 2]:
                                gc.wait("tensor", "v1", outev_done[tt % 2])
                            def two(t, tb=tb, tt=tt):
                                for nh in range(2):
                                    for cc in range(8):
                                        t.matmul(out=ps_o[tt % 2][:, nh * 512:(nh + 1) * 512],
                                                 lhsT=act[:, cc * 1024 + tt * 128: cc * 1024 + (tt + 1) * 128],
                                                 rhs=wo_sb[:, cc * 1024 + nh * 512: cc * 1024 + (nh + 1) * 512],
                                                 start=(cc == 0), stop=False)
                                    mm = t.matmul(out=ps_o[tt % 2][:, nh * 512:(nh + 1) * 512],
                                                  lhsT=cb_sb[0:1, 128:256],
                                                  rhs=bo_row[0:1, nh * 512:(nh + 1) * 512],
                                                  start=False, stop=True)
                                return mm
                            gc.tinc("tensor", two, "p1", 1)

                            gc.wait("vector", "p1", gc.v["p1"])
                            def vwo(v, tt=tt):
                                for nh in range(2):
                                    inst = v.scalar_tensor_tensor(
                                        out=out_sb[:, tt * 1024 + nh * 512: tt * 1024 + (nh + 1) * 512],
                                        in0=ps_o[tt % 2][:, nh * 512:(nh + 1) * 512],
                                        scalar=1.0, in1=x_tt[tt % 2][:, nh * 512:(nh + 1) * 512],
                                        op0=ALU.mult, op1=ALU.add)
                                return inst
                            gc.tinc("vector", vwo, "v1", 1)
                            outev_done[tt % 2] = gc.v["v1"]
                        WOEND = gc.v["p1"]

                        if debug and tb == 0:
                            gc.wait("sync", "v1", gc.v["v1"])
                            gc.dma("sync", "so", out=outinit_dbg, in_=out_sb[:])
                            gc.dma("sync", "so", out=cball_dbg, in_=cball[:])
                            gc.dma("sync", "so", out=aostg_dbg, in_=act[:, 0:8192])

                        for e in range(E):
                            for hcg in range(4):
                                gw2 = gstr_done[hcg % 2]
                                if gw2 != 0:
                                    gc.wait("sync", "s2", gw2[0])
                                    gc.wait("sync", "v2", gw2[1])
                                if hcg == 0:
                                    gc.wait("gpsimd", "p1", gc.v["p1"])
                                wsn = "wg0" if hcg % 2 == 0 else "wg1"
                                gc.dma("sync", wsn, out=g0b[hcg % 2][:],
                                       in_=g0t_s[e, hcg * 4:(hcg + 1) * 4].rearrange("h p c -> p h c"))
                                gc.dma("sync", wsn, out=g1b[hcg % 2][:],
                                       in_=g1t_s[e, hcg * 4:(hcg + 1) * 4].rearrange("h p c -> p h c"))
                                gc.dma("gpsimd", "lg",
                                       out=lw_e[:, hcg * 4096:(hcg + 1) * 4096],
                                       in_=lw[e, hcg * 512:(hcg + 1) * 512, :].rearrange("(h p) c -> p h c", p=128))
                                LWV = gc.v["lg"]
                                GLW = (wsn, gc.v[wsn])

                                for hc4 in range(4):
                                    hc = hcg * 4 + hc4
                                    par = hcg % 2
                                    for ts2 in range(2):
                                        if e == 0 and hcg == 0 and hc4 == 0:
                                            gc.wait("tensor", "v1", SCALED)
                                        gc.wait("tensor", GLW[0], GLW[1])
                                        if gelu_done[ts2]:
                                            gc.wait("tensor", "s2", gelu_done[ts2])
                                        def tg0(t, hc4=hc4, ts2=ts2, par=par):
                                            for dc in range(8):
                                                mm = t.matmul(out=ps_h0[ts2][:],
                                                              lhsT=g0b[par][:, hc4 * 1024 + dc * 128: hc4 * 1024 + (dc + 1) * 128],
                                                              rhs=xn_tb[:, dc * 1024 + ts2 * 512: dc * 1024 + (ts2 + 1) * 512],
                                                              start=(dc == 0), stop=(dc == 7))
                                            return mm
                                        gc.tinc("tensor", tg0, "p2", 1)
                                        H0P = gc.v["p2"]
                                        if act_done[ts2]:
                                            gc.wait("tensor", "v2", act_done[ts2])
                                        def tg1(t, hc4=hc4, ts2=ts2, par=par):
                                            for dc in range(8):
                                                mm = t.matmul(out=ps_h1[ts2][:],
                                                              lhsT=g1b[par][:, hc4 * 1024 + dc * 128: hc4 * 1024 + (dc + 1) * 128],
                                                              rhs=xn_tb[:, dc * 1024 + ts2 * 512: dc * 1024 + (ts2 + 1) * 512],
                                                              start=(dc == 0), stop=(dc == 7))
                                            return mm
                                        gc.tinc("tensor", tg1, "p2", 1)
                                        H1P = gc.v["p2"]

                                        gc.wait("scalar", "p2", H0P)
                                        if act_done[ts2]:
                                            gc.wait("scalar", "v2", act_done[ts2])
                                        gc.tinc("scalar", lambda s, ts2=ts2: s.activation(
                                            out=gel[ts2][:], in_=ps_h0[ts2][:], func=AF.Gelu_apprx_tanh), "s2", 1)
                                        gelu_done[ts2] = gc.v["s2"]

                                        gc.wait("vector", "p2", H1P)
                                        gc.wait("vector", "s2", gc.v["s2"])
                                        if e == 0 and hc < 8:
                                            gc.wait("vector", "p1", WOEND)
                                        def vg(v, e=e, hc=hc, ts2=ts2):
                                            v.tensor_tensor(out=tmp_b[ts2][:], in0=gel[ts2][:],
                                                            in1=ps_h1[ts2][:], op=ALU.mult)
                                            return v.tensor_tensor(out=act[:, hc * 1024 + ts2 * 512: hc * 1024 + (ts2 + 1) * 512],
                                                                   in0=tmp_b[ts2][:],
                                                                   in1=cball[:, e * 1024 + ts2 * 512: e * 1024 + (ts2 + 1) * 512],
                                                                   op=ALU.mult)
                                        gc.tinc("vector", vg, "v2", 1)
                                        act_done[ts2] = gc.v["v2"]
                                gstr_done[hcg % 2] = (gc.v["s2"], gc.v["v2"])
                            GATEV = gc.v["v2"]

                            if debug and tb == 0 and e == 0:
                                gc.wait("sync", "v2", gc.v["v2"])
                                gc.dma("sync", "so", out=act_dbg, in_=act[:])
                                gc.dma("sync", "so", out=gw_dbg[:, 0:4096], in_=g0b[0][:])
                                gc.dma("sync", "so", out=gw_dbg[:, 4096:8192], in_=g0b[1][:])

                            for tt in range(8):
                                if tt == 0:
                                    gc.wait("tensor", "v2", GATEV)
                                    gc.wait("tensor", "lg", gc.v["lg"])
                                if outev_done[tt % 2]:
                                    gc.wait("tensor", "v1", outev_done[tt % 2])
                                def tl(t, tt=tt):
                                    for nh in range(2):
                                        for hc in range(16):
                                            mm = t.matmul(out=ps_o[tt % 2][:, nh * 512:(nh + 1) * 512],
                                                          lhsT=act[:, hc * 1024 + tt * 128: hc * 1024 + (tt + 1) * 128],
                                                          rhs=lw_e[:, hc * 1024 + nh * 512: hc * 1024 + (nh + 1) * 512],
                                                          start=(hc == 0), stop=(hc == 15))
                                    return mm
                                gc.tinc("tensor", tl, "p1", 1)

                                gc.wait("vector", "p1", gc.v["p1"])
                                def vl(v, tt=tt):
                                    for nh in range(2):
                                        sl = slice(tt * 1024 + nh * 512, tt * 1024 + (nh + 1) * 512)
                                        inst = v.tensor_tensor(out=out_sb[:, sl],
                                                               in0=ps_o[tt % 2][:, nh * 512:(nh + 1) * 512],
                                                               in1=out_sb[:, sl], op=ALU.add)
                                    return inst
                                gc.tinc("vector", vl, "v1", 1)
                                outev_done[tt % 2] = gc.v["v1"]

                        if debug and tb == 0 and e == E - 1:
                            pass
                        if debug and tb == 0:
                            gc.wait("sync", "v1", gc.v["v1"])
                            gc.dma("sync", "so", out=oute0_dbg, in_=out_sb[:])
                        gc.wait("sync", "v1", gc.v["v1"])
                        gc.dma("sync", "so",
                               out=out[tb * TB:(tb + 1) * TB, :].rearrange("(tt p) c -> p tt c", p=128),
                               in_=out_sb[:])
                        SO_STORE[0] = gc.v["so"]

                multi_emit(nc, blk, sems, V, gen5)

                def f_fin(s):
                    s.wait_ge(sems["so"], p.v["so"])
                blk.sync(f_fin)

    return nc



def _consts():
    ident_b = np.eye(128, dtype=ml_dtypes.bfloat16)
    ones_b = np.ones((128, 128), dtype=ml_dtypes.bfloat16)
    kk0, qq0 = np.meshgrid(np.arange(128), np.arange(128), indexing="ij")
    tri01 = (kk0 <= qq0).astype(ml_dtypes.bfloat16)
    blkpat = np.zeros((128, 128), dtype=ml_dtypes.bfloat16)
    blkpat[0, 0:64] = 1
    blkpat[1, 64:128] = 1
    cstb = np.concatenate([ident_b, ones_b, tri01, blkpat], axis=1)
    ident_f = np.eye(128, dtype=np.float32)
    kk, qq = np.meshgrid(np.arange(128), np.arange(128), indexing="ij")
    maskT = np.where(kk <= qq, 0.0, BIGNEG).astype(np.float32)
    ones_f = np.ones((128, 128), dtype=np.float32)
    cstf = np.concatenate([ident_f, maskT, ones_f], axis=1)
    return cstb, cstf


def make_in_maps(inputs):
    cstb, cstf = _consts()
    xfull = np.ascontiguousarray(np.asarray(inputs["L0_attn_x"], dtype=np.float32))
    shared = dict(
        wq=np.ascontiguousarray(inputs["L0_attn_Wq"], dtype=np.float32),
        bq=np.ascontiguousarray(inputs["L0_attn_bq"], dtype=np.float32),
        wkd=np.ascontiguousarray(inputs["L0_attn_Wkd"], dtype=np.float32),
        bkd=np.ascontiguousarray(inputs["L0_attn_bkd"], dtype=np.float32),
        wku=np.ascontiguousarray(inputs["L0_attn_Wku"], dtype=np.float32),
        bku=np.ascontiguousarray(inputs["L0_attn_bku"], dtype=np.float32),
        wo=np.ascontiguousarray(inputs["L0_attn_Wo"], dtype=np.float32),
        bo=np.ascontiguousarray(inputs["L0_attn_bo"], dtype=np.float32),
        rw=np.ascontiguousarray(inputs["L0_ffn_router_w"], dtype=np.float32),
        rsc=np.ascontiguousarray(inputs["L0_ffn_router_scale"], dtype=np.float32),
        pes=np.ascontiguousarray(inputs["L0_ffn_per_expert_scale"], dtype=np.float32),
        gw=np.ascontiguousarray(inputs["L0_ffn_gating_w"], dtype=np.float32),
        lw=np.ascontiguousarray(inputs["L0_ffn_linear_w"], dtype=np.float32),
        cstb=cstb, cstf=cstf,
    )
    maps = []
    for i in range(NCORE):
        m = dict(shared)
        m["x"] = np.ascontiguousarray(xfull[SEQ * i:SEQ * (i + 1)].reshape(NT, C))
        maps.append(m)
    return maps


def kernel(**inputs):
    nc = build_program()
    maps = make_in_maps(inputs)
    res = run_bass_kernel_spmd(nc, maps, core_ids=list(range(NCORE)))
    outs = [r["out"].reshape(SEQ, T, C) for r in res.results]
    return np.concatenate(outs, axis=0).astype(np.float32)


# version 18
# speedup vs baseline: 1.1356x; 1.1356x over previous
import contextlib
import numpy as np
import ml_dtypes
import concourse.bass as bass
import concourse.mybir as mybir
from concourse.bass_utils import run_bass_kernel_spmd

F32 = mybir.dt.float32
BF16 = mybir.dt.bfloat16
AF = mybir.ActivationFunctionType
ALU = mybir.AluOpType
AX = mybir.AxisListType

NCORE = 8
SEQ = 2
T = 2048
NT = SEQ * T
C = 1024
H = 16
D = 64
LAT = 256
HID = 2048
E = 8
NTILE = NT // 128
TB = 1024
NTB = NT // TB

BIGNEG = -1.0e30


class Prog:

    def __init__(self, nc, blk, sems, v=None):
        self.nc = nc
        self.blk = blk
        self.sems = sems
        self.v = {k: 0 for k in sems} if v is None else v

    def inc(self, inst, s, n=1):
        inst.then_inc(self.sems[s], n)
        self.v[s] += n
        return self.v[s]

    def dinc(self, eng, s, n=1):
        eng.drain().then_inc(self.sems[s], n)
        self.v[s] += n
        return self.v[s]

    def bump(self, s, n):
        self.v[s] += n
        return self.v[s]


class GC:

    def __init__(self, sems, v, name, eng):
        self.sems = sems
        self.v = v
        self.name = name
        self.eng = eng

    def dinc(self, engname, s, n=1):
        if engname == self.name:
            self.eng.drain().then_inc(self.sems[s], n)
        self.v[s] += n
        return self.v[s]

    def minc(self, engname, inst_fn, s, n=1):
        if engname == self.name:
            inst_fn().then_inc(self.sems[s], n)
        self.v[s] += n
        return self.v[s]

    def dma(self, engname, s, out, in_, **kw):
        if engname == self.name:
            self.eng.dma_start(out=out, in_=in_, **kw).then_inc(self.sems[s], 16)
        self.v[s] += 16
        return self.v[s]

    def wait(self, engname, s, val):
        if engname == self.name and val > 0:
            self.eng.wait_ge(self.sems[s], val)

    def run(self, engname, fn):
        if engname == self.name:
            fn(self.eng)

    def tinc(self, engname, fn, s, n=1):
        if engname == self.name:
            inst = fn(self.eng)
            inst.then_inc(self.sems[s], n)
        self.v[s] += n
        return self.v[s]


def multi_emit(nc, blk, sems, V, gen):
    base = dict(V)
    final = None
    for name, sec in [("tensor", blk.tensor), ("scalar", blk.scalar),
                      ("vector", blk.vector), ("sync", blk.sync),
                      ("gpsimd", blk.gpsimd)]:
        gc = GC(sems, dict(base), name, None)

        def f(eng, gc=gc):
            gc.eng = eng
            gen(gc)
        sec(f)
        if final is None:
            final = gc.v
        else:
            assert final == gc.v, f"pass divergence: {name}"
    V.clear()
    V.update(final)


def build_program(phases=(1, 2, 3, 4, 5), debug=False):
    nc = bass.Bass("TRN2", target_bir_lowering=False)

    def dram(name, shape, dt, kind):
        return nc.dram_tensor(name, shape, dt, kind=kind).ap()

    IN, OUT_K, INT = "ExternalInput", "ExternalOutput", "Internal"
    dbg_kind = OUT_K if debug else INT

    x = dram("x", [NT, C], F32, IN)
    wq = dram("wq", [C, C], F32, IN)
    bq = dram("bq", [C], F32, IN)
    wkd = dram("wkd", [C, 2 * LAT], F32, IN)
    bkd = dram("bkd", [2 * LAT], F32, IN)
    wku = dram("wku", [LAT, 2 * C], F32, IN)
    bku = dram("bku", [2 * C], F32, IN)
    wo = dram("wo", [C, C], F32, IN)
    bo = dram("bo", [C], F32, IN)
    rw = dram("rw", [C, E], F32, IN)
    rsc = dram("rsc", [C], F32, IN)
    pes = dram("pes", [E], F32, IN)
    gw = dram("gw", [E, 2, HID, C], F32, IN)
    lw = dram("lw", [E, HID, C], F32, IN)
    cstb = dram("cstb", [128, 512], BF16, IN)
    cstf = dram("cstf", [128, 384], F32, IN)

    out = dram("out", [NT, C], F32, OUT_K)
    xn_park = dram("xn_park", [8, 128, NT], BF16, dbg_kind)
    ao_park = dram("ao_park", [8, 128, NT], BF16, dbg_kind)
    g0t_s = dram("g0t_s", [E, 16, 128, 1024], BF16, dbg_kind)
    g1t_s = dram("g1t_s", [E, 16, 128, 1024], BF16, dbg_kind)
    lw_s = dram("lw_s", [E, 16, 128, 1024], BF16, dbg_kind)
    sums_park = dram("sums_park", [SEQ, 8, 4, 2, 512], F32, dbg_kind)
    rcp_park = dram("rcp_park", [32, 512], BF16, dbg_kind)
    c_dbg = dram("c_dbg", [128, NTILE * E], F32, OUT_K) if debug else None
    lg_dbg = dram("lg_dbg", [128, NTILE * E], F32, OUT_K) if debug else None
    sp_dbg = dram("sp_dbg", [128, NTILE * 2], F32, OUT_K) if debug else None
    st_dbg = dram("st_dbg", [128, NTILE * 4], F32, OUT_K) if debug else None
    cball_dbg = dram("cball_dbg", [128, 8192], BF16, OUT_K) if debug else None
    aostg_dbg = dram("aostg_dbg", [128, 8192], BF16, OUT_K) if debug else None
    outinit_dbg = dram("outinit_dbg", [128, 8192], F32, OUT_K) if debug else None
    act_dbg = dram("act_dbg", [128, 16384], BF16, OUT_K) if debug else None
    oute0_dbg = dram("oute0_dbg", [128, 8192], F32, OUT_K) if debug else None
    gw_dbg = dram("gw_dbg", [128, 8192], BF16, OUT_K) if debug else None

    with contextlib.ExitStack() as top:
        ET = top.enter_context
        sems = {}
        for s in ["bar", "lx", "lw", "lg", "pk", "so",
                  "p1", "p2", "v1", "v2", "s1", "s2", "g1",
                  "wx0", "wx1", "wg0", "wg1", "wg2", "wg3", "wg4", "wg5",
                  "wg6", "wg7", "sg0", "sg1", "sg2", "sg3", "sg4", "sg5",
                  "sg6", "sg7", "wtb", "st0", "st1"]:
            sems[s] = ET(nc.semaphore(s))

        V = {}
        cb_sb = ET(nc.sbuf_tensor("cb_sb", [128, 512], BF16))
        cf_sb = ET(nc.sbuf_tensor("cf_sb", [128, 384], F32))
        wo_sb = ET(nc.sbuf_tensor("wo_sb", [128, 8 * C], BF16))
        bo_row = ET(nc.sbuf_tensor("bo_row", [1, C], BF16))
        c_sb = ET(nc.sbuf_tensor("c_sb", [128, NTILE * E], F32))
        c_bf = ET(nc.sbuf_tensor("c_bf", [128, NTILE * E], BF16))
        eps_sb = ET(nc.sbuf_tensor("eps_sb", [128, 4], F32))

        barv = [0]

        def barrier(p):
            engs = [p.blk.sync, p.blk.gpsimd, p.blk.tensor, p.blk.vector, p.blk.scalar]
            names = ["sync", "gpsimd", "tensor", "vector", "scalar"]
            barv[0] += 5
            tgt = barv[0]
            for sec in engs:
                def f(e):
                    e.drain().then_inc(sems["bar"], 1)
                    e.wait_ge(sems["bar"], tgt)
                sec(f)

        pw = contextlib.ExitStack()
        EW = pw.enter_context
        wq_sb = EW(nc.sbuf_tensor("wq_sb", [128, 8192], BF16))
        wkd_sb = EW(nc.sbuf_tensor("wkd_sb", [128, 4096], BF16))
        wku_sb = EW(nc.sbuf_tensor("wku_sb", [128, 4096], BF16))
        bq_sb = EW(nc.sbuf_tensor("bq_sb", [128, 8], F32))
        bkd_sb = EW(nc.sbuf_tensor("bkd_sb", [128, 4], F32))
        bkuk_sb = EW(nc.sbuf_tensor("bkuk_sb", [128, 8], F32))
        bkuv_row = EW(nc.sbuf_tensor("bkuv_row", [1, 1024], BF16))

        with contextlib.ExitStack() as ph:
            EP = ph.enter_context
            x_g = [EP(nc.sbuf_tensor(f"x_g{i}", [128, 4096], F32)) for i in range(2)]
            xn_g = EP(nc.sbuf_tensor("xn_g", [128, 4096], BF16))
            stage = [EP(nc.sbuf_tensor(f"stage{i}", [128, 4096], BF16)) for i in range(2)]
            sqdump = EP(nc.sbuf_tensor("sqdump", [128, 8 * C], F32))
            rw_b = EP(nc.sbuf_tensor("rw_b", [128, 64], BF16))
            rw_f = EP(nc.sbuf_tensor("rw_f", [128, 64], F32))
            rsc_f = EP(nc.sbuf_tensor("rsc_f", [128, 8], F32))
            pes_row = EP(nc.sbuf_tensor("pes_row", [1, 8], F32))
            pes_b = EP(nc.sbuf_tensor("pes_b", [128, 8], F32))
            sx_g = EP(nc.sbuf_tensor("sx_g", [128, 4], F32))
            ssq_g = EP(nc.sbuf_tensor("ssq_g", [128, 4], F32))
            m_g = EP(nc.sbuf_tensor("m_g", [128, 4], F32))
            ex2_g = EP(nc.sbuf_tensor("ex2_g", [128, 4], F32))
            nm2_g = EP(nc.sbuf_tensor("nm2_g", [128, 4], F32))
            var_g = EP(nc.sbuf_tensor("var_g", [128, 4], F32))
            sd_g = EP(nc.sbuf_tensor("sd_g", [128, 4], F32))
            rstd_g = EP(nc.sbuf_tensor("rstd_g", [128, 4], F32))
            t1_g = EP(nc.sbuf_tensor("t1_g", [128, 4], F32))
            vxn_g = EP(nc.sbuf_tensor("vxn_g", [128, 4], F32))
            sa_g = EP(nc.sbuf_tensor("sa_g", [128, 4], F32))
            rs_g = EP(nc.sbuf_tensor("rs_g", [128, 4], F32))
            sp_g = EP(nc.sbuf_tensor("sp_g", [128, 4], F32))
            lg8 = EP(nc.sbuf_tensor("lg8", [8, 512], F32))
            r_lg4 = EP(nc.sbuf_tensor("r_lg4", [128, 32], F32))
            r_sub = EP(nc.sbuf_tensor("r_sub", [128, 32], F32))
            r_ex4 = EP(nc.sbuf_tensor("r_ex4", [128, 32], F32))
            r_ge24 = EP(nc.sbuf_tensor("r_ge24", [128, 32], F32))
            r_nm1 = EP(nc.sbuf_tensor("r_nm1", [128, 1], F32))
            r_ge1 = EP(nc.sbuf_tensor("r_ge1", [128, 8], F32))
            r_msk = EP(nc.sbuf_tensor("r_msk", [128, 8], F32))
            r_nm2 = EP(nc.sbuf_tensor("r_nm2", [128, 1], F32))
            r_gt = EP(nc.sbuf_tensor("r_gt", [128, 8], F32))
            r_ss = EP(nc.sbuf_tensor("r_ss", [128, 1], F32))
            r_rc = EP(nc.sbuf_tensor("r_rc", [128, 1], F32))
            r_cf = EP(nc.sbuf_tensor("r_cf", [128, 8], F32))
            ps_x = [EP(nc.psum_tensor(f"ps_x{i}", [128, 1024], F32)) for i in range(2)]
            ps_lg = EP(nc.psum_tensor("ps_lg", [8, 512], F32))
            ps_lgt = EP(nc.psum_tensor("ps_lgt", [128, 32], F32))
            gbuf = [EP(nc.sbuf_tensor(f"gbuf{i}", [128, 1024], BF16)) for i in range(8)]
            tbuf = [EP(nc.sbuf_tensor(f"tbuf{i}", [128, 1024], BF16)) for i in range(8)]
            ps_t = [EP(nc.psum_tensor(f"ps_t{i}", [128, 512], F32)) for i in range(2)]
            blk = EP(nc.Block("p1"))
            p = Prog(nc, blk, sems, V or {k: 0 for k in sems})
            V.update(p.v); p.v = V
            ps_pes = ps_lgt[:, 0:8]

            def gen1(gc):
                gc.dma("gpsimd", "lg", out=cb_sb[:], in_=cstb)
                gc.dma("gpsimd", "lg", out=cf_sb[:], in_=cstf)
                gc.dma("gpsimd", "lg", out=rw_f[:], in_=rw.rearrange("(dc p) e -> p dc e", p=128))
                gc.dma("gpsimd", "lg", out=rsc_f[:], in_=rsc.rearrange("(dc p) -> p dc", p=128),
                       allow_slow_non_contiguous=True)
                gc.dma("gpsimd", "lg", out=pes_row[:], in_=pes.unsqueeze(0))
                gc.dma("gpsimd", "lg", out=wo_sb[:], in_=wo.rearrange("(dc p) c -> p dc c", p=128))
                gc.dma("gpsimd", "lg", out=bo_row[:], in_=bo.unsqueeze(0))
                LG_SMALL = gc.v["lg"]

                gc.run("vector", lambda v: (v.memset(eps_sb[:, 0:1], 1e-5),
                                            v.memset(eps_sb[:, 1:2], 1e-6)))
                gc.wait("vector", "lg", LG_SMALL)
                def rwprep(v):
                    for dc in range(8):
                        inst = v.tensor_scalar(out=rw_b[:, dc * 8:(dc + 1) * 8],
                                               in0=rw_f[:, dc * 8:(dc + 1) * 8],
                                               scalar1=rsc_f[:, dc:dc + 1], scalar2=None,
                                               op0=ALU.mult)
                    return inst
                gc.tinc("vector", rwprep, "v1", 1)

                gc.wait("tensor", "lg", LG_SMALL)
                gc.tinc("tensor", lambda t: t.matmul(out=ps_pes, lhsT=cf_sb[0:1, 256:384],
                                                     rhs=pes_row[:], start=True, stop=True), "p1", 1)
                gc.wait("vector", "p1", gc.v["p1"])
                gc.tinc("vector", lambda v: v.tensor_copy(out=pes_b[:], in_=ps_pes), "v1", 1)
                PESV = gc.v["v1"]

                xfree = {0: 0, 1: 0}
                stpark = {0: 0, 1: 0}
                evdone = {0: 0, 1: 0}
                for ts in range(8):
                    par = ts % 2
                    if xfree[par]:
                        gc.wait("sync", "v1", xfree[par])
                    sn = "wx0" if par == 0 else "wx1"
                    gc.dma("sync", sn, out=x_g[par][:],
                           in_=x[ts * 512:(ts + 1) * 512, :].rearrange("(tt p) c -> p tt c", p=128))
                    LXV = (sn, gc.v[sn])

                    gc.wait("scalar", LXV[0], LXV[1])
                    def sq4(s, par=par):
                        for tt4 in range(4):
                            inst = s.activation(out=sqdump[:, (par * 4 + tt4) * 1024:(par * 4 + tt4 + 1) * 1024],
                                                in_=x_g[par][:, tt4 * 1024:(tt4 + 1) * 1024],
                                                func=AF.Square, accum_out=ssq_g[:, tt4:tt4 + 1])
                        return inst
                    gc.tinc("scalar", sq4, "s1", 1)

                    gc.wait("vector", LXV[0], LXV[1])
                    gc.wait("vector", "s1", gc.v["s1"])
                    def stats(v, par=par):
                        for tt4 in range(4):
                            v.reduce_sum(out=sx_g[:, tt4:tt4 + 1],
                                         in_=x_g[par][:, tt4 * 1024:(tt4 + 1) * 1024], axis=AX.X)
                        v.drain()
                        v.tensor_scalar(out=m_g[:], in0=sx_g[:], scalar1=1.0 / C,
                                        scalar2=None, op0=ALU.mult)
                        v.tensor_scalar(out=ex2_g[:], in0=ssq_g[:], scalar1=1.0 / C,
                                        scalar2=None, op0=ALU.mult)
                        v.drain()
                        v.scalar_tensor_tensor(out=nm2_g[:], in0=m_g[:], scalar=-1.0,
                                               in1=m_g[:], op0=ALU.mult, op1=ALU.mult)
                        v.drain()
                        return v.tensor_tensor(out=var_g[:], in0=ex2_g[:], in1=nm2_g[:], op=ALU.add)
                    gc.tinc("vector", stats, "v1", 1)

                    gc.wait("scalar", "v1", gc.v["v1"])
                    gc.tinc("scalar", lambda s: s.activation(out=sd_g[:], in_=var_g[:], func=AF.Sqrt,
                                                             bias=eps_sb[:, 0:1]), "s1", 1)
                    gc.wait("vector", "s1", gc.v["s1"])
                    def mid(v):
                        v.reciprocal(out=rstd_g[:], in_=sd_g[:])
                        v.drain()
                        v.tensor_tensor(out=t1_g[:], in0=var_g[:], in1=rstd_g[:], op=ALU.mult)
                        v.drain()
                        return v.tensor_tensor(out=vxn_g[:], in0=t1_g[:], in1=rstd_g[:], op=ALU.mult)
                    gc.tinc("vector", mid, "v1", 1)

                    gc.wait("scalar", "v1", gc.v["v1"])
                    gc.tinc("scalar", lambda s: s.activation(out=sa_g[:], in_=vxn_g[:], func=AF.Sqrt,
                                                             bias=eps_sb[:, 1:2]), "s1", 1)
                    gc.wait("vector", "s1", gc.v["s1"])
                    def fin(v, par=par):
                        v.reciprocal(out=rs_g[:], in_=sa_g[:])
                        v.drain()
                        v.tensor_scalar(out=sp_g[:], in0=rs_g[:], scalar1=1.0 / 32.0,
                                        scalar2=None, op0=ALU.mult)
                        for tt4 in range(4):
                            inst = v.tensor_scalar(out=xn_g[:, tt4 * 1024:(tt4 + 1) * 1024],
                                                   in0=x_g[par][:, tt4 * 1024:(tt4 + 1) * 1024],
                                                   scalar1=m_g[:, tt4:tt4 + 1],
                                                   scalar2=rstd_g[:, tt4:tt4 + 1],
                                                   op0=ALU.subtract, op1=ALU.mult)
                        return inst
                    gc.tinc("vector", fin, "v1", 1)
                    xfree[par] = gc.v["v1"]
                    XNV = gc.v["v1"]

                    evs = []
                    for tt4 in range(4):
                        pp = tt4 % 2
                        gc.wait("tensor", "v1", XNV)
                        if evdone[pp]:
                            gc.wait("tensor", "v1", evdone[pp])
                        def tx(t, tt4=tt4, pp=pp):
                            for dc in range(8):
                                mm = t.matmul(out=ps_x[pp][:, dc * 128:(dc + 1) * 128],
                                              lhsT=xn_g[:, tt4 * 1024 + dc * 128: tt4 * 1024 + (dc + 1) * 128],
                                              rhs=cb_sb[:, 0:128], start=True, stop=True)
                            return mm
                        gc.tinc("tensor", tx, "p1", 1)
                        gc.wait("vector", "p1", gc.v["p1"])
                        if tt4 == 0 and stpark[par]:
                            gc.wait("vector", "st0" if par == 0 else "st1", stpark[par])
                        def ev(v, tt4=tt4, pp=pp, par=par):
                            return v.tensor_copy(
                                out=stage[par][:].rearrange("p (d t) -> p d t", d=8)[:, :, tt4 * 128:(tt4 + 1) * 128],
                                in_=ps_x[pp][:].rearrange("p (d t) -> p d t", d=8))
                        gc.tinc("vector", ev, "v1", 1)
                        evdone[pp] = gc.v["v1"]
                        evs.append(gc.v["v1"])

                    STGV = gc.v["v1"]
                    gc.wait("tensor", "v1", STGV)
                    def tlg(t, par=par):
                        for dc in range(8):
                            mm = t.matmul(out=ps_lg[:], lhsT=rw_b[:, dc * 8:(dc + 1) * 8],
                                          rhs=stage[par][:, dc * 512:(dc + 1) * 512],
                                          start=(dc == 0), stop=(dc == 7))
                        return mm
                    gc.tinc("tensor", tlg, "p1", 1)
                    gc.wait("vector", "p1", gc.v["p1"])
                    gc.tinc("vector", lambda v: v.tensor_copy(out=lg8[:], in_=ps_lg[:]), "v1", 1)
                    gc.wait("tensor", "v1", gc.v["v1"])
                    def tlgt(t):
                        for tt4 in range(4):
                            mm = t.matmul(out=ps_lgt[:, tt4 * 8:(tt4 + 1) * 8],
                                          lhsT=lg8[:, tt4 * 128:(tt4 + 1) * 128],
                                          rhs=cf_sb[0:8, 0:8], start=True, stop=True)
                        return mm
                    gc.tinc("tensor", tlgt, "p1", 1)
                    LGTP = gc.v["p1"]

                    gc.wait("vector", "p1", LGTP)
                    if ts > 0:
                        gc.wait("vector", "s1", gc.v["s1"])
                    def rpre(v, ts=ts):
                        for tt4 in range(4):
                            sl = slice(tt4 * 8, (tt4 + 1) * 8)
                            v.tensor_scalar(out=r_lg4[:, sl], in0=ps_lgt[:, sl],
                                            scalar1=sp_g[:, tt4:tt4 + 1], scalar2=None, op0=ALU.mult)
                            v.drain()
                            v.reduce_max(out=r_nm1[:], in_=r_lg4[:, sl], axis=AX.X, negate=True)
                            v.drain()
                            v.tensor_scalar(out=r_sub[:, sl], in0=r_lg4[:, sl], scalar1=r_nm1[:],
                                            scalar2=None, op0=ALU.add)
                            v.tensor_scalar(out=r_ge1[:], in0=r_lg4[:, sl], scalar1=r_nm1[:],
                                            scalar2=0.0, op0=ALU.add, op1=ALU.is_ge)
                            v.drain()
                            v.scalar_tensor_tensor(out=r_msk[:], in0=r_ge1[:], scalar=BIGNEG,
                                                   in1=r_lg4[:, sl], op0=ALU.mult, op1=ALU.add)
                            v.drain()
                            v.reduce_max(out=r_nm2[:], in_=r_msk[:], axis=AX.X, negate=True)
                            v.drain()
                            inst = v.tensor_scalar(out=r_ge24[:, sl], in0=r_lg4[:, sl], scalar1=r_nm2[:],
                                                   scalar2=0.0, op0=ALU.add, op1=ALU.is_ge)
                        return inst
                    gc.tinc("vector", rpre, "v2", 1)
                    gc.wait("scalar", "v2", gc.v["v2"])
                    gc.tinc("scalar", lambda s: s.activation(out=r_ex4[:], in_=r_sub[:], func=AF.Exp),
                            "s1", 1)
                    gc.wait("vector", "s1", gc.v["s1"])
                    def rpost(v, ts=ts):
                        for tt4 in range(4):
                            sl = slice(tt4 * 8, (tt4 + 1) * 8)
                            tcol = (ts * 4 + tt4) * 8
                            v.tensor_tensor(out=r_gt[:], in0=r_ex4[:, sl], in1=r_ge24[:, sl], op=ALU.mult)
                            v.drain()
                            v.reduce_sum(out=r_ss[:], in_=r_gt[:], axis=AX.X)
                            v.drain()
                            v.reciprocal(out=r_rc[:], in_=r_ss[:])
                            v.drain()
                            v.tensor_scalar(out=r_cf[:], in0=r_gt[:], scalar1=r_rc[:],
                                            scalar2=None, op0=ALU.mult)
                            v.drain()
                            v.tensor_tensor(out=c_sb[:, tcol:tcol + 8], in0=r_cf[:],
                                            in1=pes_b[:], op=ALU.mult)
                            v.drain()
                            inst = v.tensor_copy(out=c_bf[:, tcol:tcol + 8], in_=c_sb[:, tcol:tcol + 8])
                        return inst
                    gc.tinc("vector", rpost, "v2", 1)

                    gc.wait("sync", "v1", STGV)
                    psn = "st0" if par == 0 else "st1"
                    gc.dma("sync", psn, out=xn_park[:, :, ts * 512:(ts + 1) * 512].rearrange("d p t -> p d t"),
                           in_=stage[par][:])
                    stpark[par] = gc.v[psn]

                if debug:
                    gc.wait("sync", "v2", gc.v["v2"])
                    gc.dma("sync", "so", out=c_dbg, in_=c_sb[:])
                gc.wait("sync", "st0", gc.v["st0"])
                gc.wait("sync", "st1", gc.v["st1"])
                if gc.v["so"]:
                    gc.wait("sync", "so", gc.v["so"])

            multi_emit(nc, blk, sems, V, gen1)

            if 2 in phases:
                WG_BASE = {i: p.v[f"wg{i}"] for i in range(8)}
                SG_BASE = {i: p.v[f"sg{i}"] for i in range(8)}
                P2_0 = p.v["p1"]
                V2_0 = p.v["v1"]
                S2_0 = p.v["s1"]

                def g_attnw(g):
                    p.inc(g.dma_start(out=wq_sb[:], in_=wq.rearrange("(dc p) c -> p dc c", p=128)), "lw", 16)
                    p.inc(g.dma_start(out=wkd_sb[:], in_=wkd.rearrange("(dc p) f -> p dc f", p=128)), "lw", 16)
                    p.inc(g.dma_start(out=wku_sb[:], in_=wku.rearrange("(lc p) f -> p lc f", p=128)), "lw", 16)
                    p.inc(g.dma_start(out=bq_sb[:], in_=bq.rearrange("(cc p) -> p cc", p=128),
                                      allow_slow_non_contiguous=True), "lw", 16)
                    p.inc(g.dma_start(out=bkd_sb[:], in_=bkd.rearrange("(lc p) -> p lc", p=128),
                                      allow_slow_non_contiguous=True), "lw", 16)
                    p.inc(g.dma_start(out=bkuk_sb[:], in_=bku[0:1024].rearrange("(cc p) -> p cc", p=128),
                                      allow_slow_non_contiguous=True), "lw", 16)
                    p.inc(g.dma_start(out=bkuv_row[:], in_=bku[1024:2048].unsqueeze(0)), "lw", 16)
                blk.gpsimd(g_attnw)

                PB = [ps_t[0][:], ps_t[1][:],
                      ps_x[0][:, 0:512], ps_x[0][:, 512:1024],
                      ps_x[1][:, 0:512], ps_x[1][:, 512:1024]]

                NU = E * 2 * 16
                for u in range(NU):
                    e, gx, hc = u // 32, (u // 16) % 2, u % 16
                    par = u % 8
                    gsrc = gw[e, gx, hc * 128:(hc + 1) * 128, :]
                    gdst = (g0t_s if gx == 0 else g1t_s)[e, hc]

                    def f_g(g, u=u, par=par, gsrc=gsrc):
                        if u >= 8:
                            g.wait_ge(sems["p1"], P2_0 + 2 * (u - 8) + 2)
                        p.inc(g.dma_start(out=gbuf[par][:], in_=gsrc), f"wg{par}", 16)
                    blk.gpsimd(f_g)

                    def f_t(t, u=u, par=par):
                        t.wait_ge(sems[f"wg{par}"], WG_BASE[par] + 16 * (u // 8 + 1))
                        for hf in range(2):
                            uh = 2 * u + hf
                            if uh >= 6:
                                if uh % 2 == 0:
                                    t.wait_ge(sems["v1"], V2_0 + (uh - 6) // 2 + 1)
                                else:
                                    t.wait_ge(sems["s1"], S2_0 + (uh - 5) // 2 + 1)
                            elif uh % 6 >= 2:
                                t.wait_ge(sems["v1"], V2_0)
                            mm = None
                            for dc in range(4):
                                mm = t.matmul(out=PB[uh % 6][:, dc * 128:(dc + 1) * 128],
                                              lhsT=gbuf[par][:, hf * 512 + dc * 128: hf * 512 + (dc + 1) * 128],
                                              rhs=cb_sb[:, 0:128], start=True, stop=True)
                            p.inc(mm, "p1", 1)
                    blk.tensor(f_t)

                    def f_v(v, u=u, par=par):
                        v.wait_ge(sems["p1"], P2_0 + 2 * u + 1)
                        if u >= 8:
                            v.wait_ge(sems[f"sg{par}"], SG_BASE[par] + 16 * (u // 8))
                        inst = v.tensor_copy(out=tbuf[par][:, 0:512], in_=PB[(2 * u) % 6])
                        p.inc(inst, "v1", 1)
                    blk.vector(f_v)

                    def f_sc(s, u=u, par=par):
                        s.wait_ge(sems["p1"], P2_0 + 2 * u + 2)
                        if u >= 8:
                            s.wait_ge(sems[f"sg{par}"], SG_BASE[par] + 16 * (u // 8))
                        inst = s.activation(out=tbuf[par][:, 512:1024], in_=PB[(2 * u + 1) % 6],
                                            func=AF.Identity)
                        p.inc(inst, "s1", 1)
                    blk.scalar(f_sc)

                    def f_s(s, u=u, par=par, gdst=gdst):
                        s.wait_ge(sems["v1"], V2_0 + u + 1)
                        s.wait_ge(sems["s1"], S2_0 + u + 1)
                        p.inc(s.dma_start(out=gdst, in_=tbuf[par][:]), f"sg{par}", 16)
                    blk.sync(f_s)

                def f_end2(s):
                    for i in range(8):
                        s.wait_ge(sems[f"sg{i}"], p.v[f"sg{i}"])
                blk.sync(f_end2)

                def f_end2g(g):
                    if p.v["pk"]:
                        g.wait_ge(sems["pk"], p.v["pk"])
                    g.wait_ge(sems["lw"], p.v["lw"])
                blk.gpsimd(f_end2g)
            barrier(p)

        if 3 in phases:
            with contextlib.ExitStack() as ph:
                EP = ph.enter_context
                xts = [EP(nc.sbuf_tensor(f"xts{i}", [128, 4096], BF16)) for i in range(2)]
                qf = EP(nc.sbuf_tensor("qf", [128, 16384], BF16))
                kf = EP(nc.sbuf_tensor("kf", [128, 16384], BF16))
                va = EP(nc.sbuf_tensor("va", [128, 16896], BF16))
                latf = EP(nc.sbuf_tensor("latf", [128, 4096], BF16))
                kvl = EP(nc.sbuf_tensor("kvl", [128, 2048], F32))
                expT = [EP(nc.sbuf_tensor(f"expT{i}", [128, 512], BF16)) for i in range(6)]
                qm2 = EP(nc.sbuf_tensor("qm2", [128, 8], F32))
                km2 = EP(nc.sbuf_tensor("km2", [128, 8], F32))
                nrm_t = EP(nc.sbuf_tensor("nrm_t", [128, 1], F32))
                n_a = EP(nc.sbuf_tensor("n_a", [1, 1], F32))
                n_b = EP(nc.sbuf_tensor("n_b", [1, 1], F32))
                n_c = EP(nc.sbuf_tensor("n_c", [1, 1], F32))
                n_bf = EP(nc.sbuf_tensor("n_bf", [1, 1], BF16))
                sbias = EP(nc.sbuf_tensor("sbias", [128, 16], F32))
                rr_sb = EP(nc.sbuf_tensor("rr_sb", [128, 1024], F32))
                rr_bf = EP(nc.sbuf_tensor("rr_bf", [128, 1024], BF16))
                rb_sb = EP(nc.sbuf_tensor("rb_sb", [64, 1024], F32))
                ao_sb = [EP(nc.sbuf_tensor(f"ao_sb{i}", [64, 512], BF16)) for i in range(2)]
                ps_sc = [EP(nc.psum_tensor(f"ps_sc{i}", [128, 512], F32)) for i in range(6)]
                ps_oa = [EP(nc.psum_tensor(f"ps_oa{i}", [65, 512], F32)) for i in range(2)]
                ps_pj = [ps_sc[0], ps_sc[1]]
                ps_n1 = ps_sc[0][0:1, 0:1]
                ps_n2 = ps_sc[0][0:1, 4:5]
                ps_sbias = ps_sc[1][:, 0:1]
                ps_rb = ps_sc[4][0:64, 0:512]
                ps_rb2 = ps_sc[5][0:64, 0:512]
                blk = EP(nc.Block("p3"))
                p = Prog(nc, blk, sems, V)

                def gen3(gc):
                    for s_i in range(SEQ):
                        soff = s_i * T
                        PJ0 = gc.v["p1"]
                        SE0 = gc.v["s1"]
                        LX0 = gc.v["lx"]

                        gc.wait("vector", "v2", gc.v["v2"])
                        gc.run("vector", lambda v: v.memset(
                            va[:].rearrange("p (kb h o) -> p kb h o", kb=16, h=16)[:, :, :, 64:65], 1.0))
                        gc.dinc("vector", "v1", 1)
                        VA1 = gc.v["v1"]

                        pj = [0]
                        pj_at_ts = {}
                        lxv = {}

                        def proj_group(mm_fn, evac_eng, evac_fn, extra_wait=None):
                            g = pj[0]
                            par = g % 2
                            if g == 0 and s_i > 0:
                                gc.wait("tensor", "v2", gc.v["v2"])
                            if extra_wait is not None:
                                gc.wait("tensor", extra_wait[0], extra_wait[1])
                            if g >= 2:
                                gc.wait("tensor", "s1", SE0 + g - 1)
                            gc.tinc("tensor", lambda t: mm_fn(t, par), "p1", 1)
                            gc.wait(evac_eng, "p1", PJ0 + g + 1)
                            gc.tinc(evac_eng, lambda e: evac_fn(e, par), "s1", 1)
                            pj[0] += 1

                        for ts in range(4):
                            if ts >= 2:
                                gc.wait("sync", "p1", PJ0 + pj_at_ts[ts - 2])
                            sn = "wx0" if ts % 2 == 0 else "wx1"
                            gc.dma("sync", sn, out=xts[ts % 2][:],
                                   in_=xn_park[:, :, soff + ts * 512: soff + (ts + 1) * 512].rearrange("d p t -> p d t"))
                            lxv[ts] = (sn, gc.v[sn])

                            for cc in range(8):
                                def mm_q(t, par, ts=ts, cc=cc):
                                    t.wait_ge(sems[lxv[ts][0]], lxv[ts][1])
                                    for dc in range(8):
                                        mm = t.matmul(out=ps_pj[par][:],
                                                 lhsT=wq_sb[:, dc * 1024 + cc * 128: dc * 1024 + (cc + 1) * 128],
                                                 rhs=xts[ts % 2][:, dc * 512:(dc + 1) * 512],
                                                 start=(dc == 0), stop=(dc == 7))
                                    return mm

                                def ev_q(s, par, ts=ts, cc=cc):
                                    return s.activation(out=qf[:, cc * 2048 + ts * 512: cc * 2048 + (ts + 1) * 512],
                                                        in_=ps_pj[par][:], func=AF.Identity,
                                                        bias=bq_sb[:, cc:cc + 1])
                                proj_group(mm_q, "scalar", ev_q)

                            for lc4 in range(4):
                                def mm_kv(t, par, ts=ts, lc4=lc4):
                                    t.wait_ge(sems[lxv[ts][0]], lxv[ts][1])
                                    for dc in range(8):
                                        mm = t.matmul(out=ps_pj[par][:],
                                                 lhsT=wkd_sb[:, dc * 512 + lc4 * 128: dc * 512 + (lc4 + 1) * 128],
                                                 rhs=xts[ts % 2][:, dc * 512:(dc + 1) * 512],
                                                 start=(dc == 0), stop=(dc == 7))
                                    return mm

                                def ev_kv(s, par, ts=ts, lc4=lc4, VA1=VA1):
                                    if lc4 == 0:
                                        s.wait_ge(sems["v1"], VA1 + 2 * ts)
                                    return s.activation(out=kvl[:, lc4 * 512:(lc4 + 1) * 512],
                                                        in_=ps_pj[par][:], func=AF.Gelu,
                                                        bias=bkd_sb[:, lc4:lc4 + 1])
                                proj_group(mm_kv, "scalar", ev_kv)

                            gc.wait("vector", "s1", SE0 + pj[0])
                            def lat_add(v, ts=ts):
                                for lc in range(2):
                                    inst = v.tensor_tensor(out=latf[:, lc * 2048 + ts * 512: lc * 2048 + (ts + 1) * 512],
                                                           in0=kvl[:, lc * 512:(lc + 1) * 512],
                                                           in1=kvl[:, (2 + lc) * 512:(3 + lc) * 512], op=ALU.add)
                                return inst
                            gc.tinc("vector", lat_add, "v1", 2)
                            LATV = gc.v["v1"]

                            for cc in range(8):
                                def mm_k(t, par, ts=ts, cc=cc):
                                    for lc in range(2):
                                        mm = t.matmul(out=ps_pj[par][:],
                                                 lhsT=wku_sb[:, lc * 2048 + cc * 128: lc * 2048 + (cc + 1) * 128],
                                                 rhs=latf[:, lc * 2048 + ts * 512: lc * 2048 + (ts + 1) * 512],
                                                 start=(lc == 0), stop=(lc == 1))
                                    return mm

                                def ev_k(s, par, ts=ts, cc=cc):
                                    return s.activation(out=kf[:, cc * 2048 + ts * 512: cc * 2048 + (ts + 1) * 512],
                                                        in_=ps_pj[par][:], func=AF.Identity,
                                                        bias=bkuk_sb[:, cc:cc + 1])
                                proj_group(mm_k, "scalar", ev_k, extra_wait=("v1", LATV))

                            for tt4 in range(4):
                                for dh in range(2):
                                    def mm_v(t, par, ts=ts, tt4=tt4, dh=dh):
                                        for lc in range(2):
                                            t.matmul(out=ps_pj[par][:],
                                                     lhsT=latf[:, lc * 2048 + ts * 512 + tt4 * 128: lc * 2048 + ts * 512 + (tt4 + 1) * 128],
                                                     rhs=wku_sb[:, lc * 2048 + 1024 + dh * 512: lc * 2048 + 1024 + (dh + 1) * 512],
                                                     start=(lc == 0), stop=False)
                                        mm = t.matmul(out=ps_pj[par][:],
                                                      lhsT=cb_sb[0:1, 128:256],
                                                      rhs=bkuv_row[0:1, dh * 512:(dh + 1) * 512],
                                                      start=False, stop=True)
                                        return mm

                                    def ev_v(v, par, ts=ts, tt4=tt4, dh=dh):
                                        kb = ts * 4 + tt4
                                        return v.tensor_copy(
                                            out=va[:].rearrange("p (kb h o) -> p kb h o", kb=16, h=16)[:, kb, dh * 8:(dh + 1) * 8, 0:64],
                                            in_=ps_pj[par][:].rearrange("p (h d) -> p h d", h=8))
                                    proj_group(mm_v, "vector", ev_v, extra_wait=("v1", LATV))

                            pj_at_ts[ts] = pj[0]

                        gc.wait("vector", "s1", SE0 + pj[0])
                        def vnorm(v):
                            for cc in range(8):
                                v.reduce_max(out=nrm_t[:], in_=qf[:, cc * 2048:(cc + 1) * 2048],
                                             axis=AX.X, apply_absolute_value=True)
                                v.drain()
                                v.tensor_scalar(out=qm2[:, cc:cc + 1], in0=nrm_t[:],
                                                scalar1=nrm_t[:], scalar2=None, op0=ALU.mult)
                                v.reduce_max(out=nrm_t[:], in_=kf[:, cc * 2048:(cc + 1) * 2048],
                                             axis=AX.X, apply_absolute_value=True)
                                v.drain()
                                v.tensor_scalar(out=km2[:, cc:cc + 1], in0=nrm_t[:],
                                                scalar1=nrm_t[:], scalar2=None, op0=ALU.mult)
                        gc.run("vector", vnorm)
                        gc.dinc("vector", "v1", 1)
                        NRMV = gc.v["v1"]

                        for h in range(16):
                            cc, odd = h // 2, h % 2
                            base = 64 * odd

                            gc.wait("tensor", "v1", NRMV + 3 * h)
                            def tn(t, cc=cc, base=base):
                                tp_ = (base, 0)
                                t.matmul(out=ps_n1, lhsT=cf_sb[base:base + 64, 256:257],
                                         rhs=qm2[base:base + 64, cc:cc + 1], start=True, stop=True,
                                         tile_position=tp_)
                                t.matmul(out=ps_n2, lhsT=cf_sb[base:base + 64, 256:257],
                                         rhs=km2[base:base + 64, cc:cc + 1], start=True, stop=True,
                                         tile_position=tp_)
                            gc.run("tensor", tn)
                            gc.dinc("tensor", "p2", 1)

                            gc.wait("vector", "p2", gc.v["p2"])
                            def vn1(v):
                                v.tensor_copy(out=n_a[:], in_=ps_n1)
                                v.drain()
                                v.tensor_scalar(out=n_b[:], in0=ps_n2, scalar1=n_a[:],
                                                scalar2=None, op0=ALU.mult)
                            gc.run("vector", vn1)
                            gc.dinc("vector", "v1", 1)

                            gc.wait("scalar", "v1", gc.v["v1"])
                            gc.run("scalar", lambda s: s.activation(out=n_c[:], in_=n_b[:], func=AF.Sqrt))
                            gc.dinc("scalar", "s1", 1)

                            gc.wait("vector", "s1", gc.v["s1"])
                            gc.run("vector", lambda v: v.tensor_scalar(
                                out=n_bf[:], in0=n_c[:], scalar1=-0.125, scalar2=None, op0=ALU.mult))
                            gc.dinc("vector", "v1", 1)

                            gc.wait("tensor", "v1", gc.v["v1"])
                            gc.run("tensor", lambda t: t.matmul(
                                out=ps_sbias, lhsT=cb_sb[0:1, 128:256],
                                rhs=n_bf[:], start=True, stop=True))
                            gc.dinc("tensor", "p2", 1)

                            gc.wait("vector", "p2", gc.v["p2"])
                            gc.run("vector", lambda v, h=h: v.tensor_copy(out=sbias[:, h:h + 1], in_=ps_sbias))
                            gc.dinc("vector", "v1", 1)

                        SBV = gc.v["v1"]

                        pk_last = [0]
                        for hp in range(8):
                            for qc in range(4):
                                nkb = 4 * qc + 4
                                QK0 = gc.v["p2"]
                                MV0 = gc.v["v2"]
                                EX0 = gc.v["s2"]
                                mask_v2 = {}

                                qk_p2 = {}
                                av_p2 = {}

                                def emit_qk(kb):
                                    par = kb % 3
                                    if hp == 0 and qc == 0 and kb == 0:
                                        gc.wait("tensor", "v1", SBV)
                                    if kb < 3:
                                        gc.wait("tensor", "s2", EX0)
                                    else:
                                        gc.wait("tensor", "s2", EX0 + (kb - 2))
                                    if kb == 2:
                                        gc.wait("tensor", "v2", MV0)
                                    def qk(t, hp=hp, qc=qc, kb=kb, par=par):
                                        for odd in range(2):
                                            bb = 64 * odd
                                            mm = t.matmul(out=ps_sc[2 * par + odd][:],
                                                          lhsT=kf[bb:bb + 64, hp * 2048 + kb * 128: hp * 2048 + (kb + 1) * 128],
                                                          rhs=qf[bb:bb + 64, hp * 2048 + qc * 512: hp * 2048 + (qc + 1) * 512],
                                                          start=True, stop=True, tile_position=(bb, 0))
                                        return mm
                                    gc.tinc("tensor", qk, "p2", 1)
                                    qk_p2[kb] = gc.v["p2"]

                                def emit_exp(kb):
                                    par = kb % 3
                                    lead = kb * 128 - qc * 512
                                    gc.wait("scalar", "p2", qk_p2[kb])
                                    if kb >= 3:
                                        gc.wait("scalar", "p2", av_p2[kb - 3])
                                        if (kb - 3) in mask_v2:
                                            gc.wait("scalar", "v2", mask_v2[kb - 3])
                                    def ex(s, hp=hp, kb=kb, par=par):
                                        for odd in range(2):
                                            inst = s.activation(out=expT[2 * par + odd][:],
                                                                in_=ps_sc[2 * par + odd][:], func=AF.Exp,
                                                                scale=0.125, bias=sbias[:, 2 * hp + odd:2 * hp + odd + 1])
                                        return inst
                                    gc.tinc("scalar", ex, "s2", 1)
                                    if lead >= 0:
                                        gc.wait("vector", "s2", EX0 + kb + 1)
                                        def vm(v, kb=kb, par=par, lead=lead):
                                            for odd in range(2):
                                                et = expT[2 * par + odd]
                                                if lead > 0:
                                                    v.memset(et[:, 0:lead], 0.0)
                                                inst = v.tensor_tensor(out=et[:, lead:lead + 128],
                                                                       in0=et[:, lead:lead + 128],
                                                                       in1=cb_sb[:, 256:384], op=ALU.mult)
                                            return inst
                                        gc.tinc("vector", vm, "v2", 1)
                                        mask_v2[kb] = gc.v["v2"]

                                def emit_av(kb):
                                    par = kb % 3
                                    gc.wait("tensor", "s2", EX0 + kb + 1)
                                    if kb in mask_v2:
                                        gc.wait("tensor", "v2", mask_v2[kb])
                                    if kb == 0:
                                        gc.wait("tensor", "v2", MV0)
                                    def av(t, hp=hp, kb=kb, par=par, nkb=nkb):
                                        for odd in range(2):
                                            h = 2 * hp + odd
                                            mm = t.matmul(out=ps_oa[odd][:],
                                                          lhsT=va[:, kb * 1056 + h * 66: kb * 1056 + h * 66 + 65],
                                                          rhs=expT[2 * par + odd][:],
                                                          start=(kb == 0), stop=(kb == nkb - 1))
                                        return mm
                                    gc.tinc("tensor", av, "p2", 1)
                                    av_p2[kb] = gc.v["p2"]

                                for kb in range(nkb):
                                    emit_qk(kb)
                                    emit_exp(kb)
                                    if kb >= 2:
                                        emit_av(kb - 2)
                                emit_av(nkb - 2)
                                emit_av(nkb - 1)

                                AV_END = gc.v["p2"]
                                gc.wait("vector", "p2", AV_END)
                                if pk_last[0] > 0:
                                    gc.wait("vector", "pk", pk_last[0])
                                def vao(v):
                                    v.tensor_copy(out=rr_sb[64:65, 0:512], in_=ps_oa[0][64:65, :])
                                    v.tensor_copy(out=rr_sb[64:65, 512:1024], in_=ps_oa[1][64:65, :])
                                    v.tensor_copy(out=ao_sb[0][:], in_=ps_oa[0][0:64, :])
                                    return v.tensor_copy(out=ao_sb[1][:], in_=ps_oa[1][0:64, :])
                                gc.tinc("vector", vao, "v2", 1)

                                gc.wait("gpsimd", "v2", gc.v["v2"])
                                for odd in range(2):
                                    gc.dma("gpsimd", "pk",
                                           out=ao_park[hp, odd * 64:(odd + 1) * 64,
                                                       s_i * T + qc * 512:s_i * T + (qc + 1) * 512],
                                           in_=ao_sb[odd][:])
                                gc.dma("gpsimd", "pk",
                                       out=sums_park[s_i, hp, qc].rearrange("o t -> (o t)").unsqueeze(0),
                                       in_=rr_sb[64:65, :])
                                pk_last[0] = gc.v["pk"]

                multi_emit(nc, blk, sems, V, gen3)

                def f_end3(g):
                    g.wait_ge(sems["pk"], p.v["pk"])
                blk.gpsimd(f_end3)
                barrier(p)
        pw.close()

        if 5 in phases:
            with contextlib.ExitStack() as ph:
                EP = ph.enter_context
                xn_tb = EP(nc.sbuf_tensor("xn_tb", [128, 8192], BF16))
                x_tt = [EP(nc.sbuf_tensor(f"x_tt{i}", [128, 1024], F32)) for i in range(2)]
                out_sb = EP(nc.sbuf_tensor("out_sb", [128, 8192], F32))
                act = EP(nc.sbuf_tensor("act", [128, 16384], BF16))
                gel = [EP(nc.sbuf_tensor(f"gel{i}", [128, 512], BF16)) for i in range(2)]
                tmp_b = [EP(nc.sbuf_tensor(f"tmp_b{i}", [128, 512], BF16)) for i in range(2)]
                g0b = [EP(nc.sbuf_tensor(f"g0b{i}", [128, 4096], BF16)) for i in range(2)]
                g1b = [EP(nc.sbuf_tensor(f"g1b{i}", [128, 4096], BF16)) for i in range(2)]
                lw_e = EP(nc.sbuf_tensor("lw_e", [128, 16384], BF16))
                cball = EP(nc.sbuf_tensor("cball", [128, 8192], BF16))
                ctrow = EP(nc.sbuf_tensor("ctrow", [1, 1024], BF16))
                sums2 = EP(nc.sbuf_tensor("sums2", [32, 512], F32))
                rcp_b = EP(nc.sbuf_tensor("rcp_b", [32, 512], BF16))
                rcp2 = EP(nc.sbuf_tensor("rcp2", [2, 4096], BF16))
                ps_h0 = [EP(nc.psum_tensor(f"ps_h0{i}", [128, 512], F32)) for i in range(2)]
                ps_h1 = [EP(nc.psum_tensor(f"ps_h1{i}", [128, 512], F32)) for i in range(2)]
                ps_o = [EP(nc.psum_tensor(f"ps_o{i}", [128, 1024], F32)) for i in range(2)]
                blk = EP(nc.Block("p5"))
                p = Prog(nc, blk, sems, V)

                def gen5(gc):
                    SO_STORE = [0]
                    gelu_done = {0: 0, 1: 0}
                    act_done = {0: 0, 1: 0}
                    outev_done = {0: 0, 1: 0}
                    gstr_done = {0: 0, 1: 0}
                    RCPV = [0]
                    PKV = [0]
                    CHP = [0]
                    ST0P = [0]

                    for tb in range(NTB):
                        if tb >= 1:
                            gc.wait("sync", "p1", gc.v["p1"])
                            gc.wait("sync", "p2", gc.v["p2"])
                            gc.wait("sync", "v1", RCPV[0])
                        gc.dma("sync", "wtb", out=xn_tb[:],
                               in_=xn_park[:, :, tb * TB:(tb + 1) * TB].rearrange("d p t -> p d t"))
                        gc.dma("sync", "wtb", out=act[:, 0:8192],
                               in_=ao_park[:, :, tb * TB:(tb + 1) * TB].rearrange("d p t -> p d t"))
                        s_i5, q01 = tb // 2, (tb % 2) * 2
                        for q2 in range(2):
                            for o2 in range(2):
                                r0 = q2 * 16 + o2 * 8
                                gc.dma("sync", "wtb", out=sums2[r0:r0 + 8],
                                       in_=sums_park[s_i5, :, q01 + q2, o2])
                        TBLW = gc.v["wtb"]

                        gc.wait("vector", "wtb", TBLW)
                        if PKV[0]:
                            gc.wait("vector", "pk", PKV[0])
                        def vrcp(v):
                            v.reciprocal(out=sums2[:], in_=sums2[:])
                            v.drain()
                            return v.tensor_copy(out=rcp_b[:], in_=sums2[:])
                        gc.tinc("vector", vrcp, "v1", 1)
                        RCPV[0] = gc.v["v1"]
                        gc.wait("gpsimd", "v1", RCPV[0])
                        if ST0P[0]:
                            gc.wait("gpsimd", "lx", ST0P[0])
                        gc.dma("gpsimd", "pk", out=rcp_park, in_=rcp_b[:])
                        PKV[0] = gc.v["pk"]

                        for e in range(E):
                            if e == 0:
                                if tb >= 1:
                                    gc.wait("tensor", "v1", gc.v["v1"])
                                    gc.wait("tensor", "s2", gc.v["s2"])
                                    gc.wait("tensor", "v2", gc.v["v2"])
                            else:
                                gc.wait("tensor", "v1", gc.v["v1"])
                            def tct(t, tb=tb, e=e):
                                for tt in range(8):
                                    mm = t.matmul(out=ps_h0[tt // 4][0:1, (tt % 4) * 128:(tt % 4 + 1) * 128],
                                                  lhsT=c_bf[:, (tb * 8 + tt) * 8 + e:(tb * 8 + tt) * 8 + e + 1],
                                                  rhs=cb_sb[:, 0:128], start=True, stop=True)
                                return mm
                            gc.tinc("tensor", tct, "p1", 1)

                            gc.wait("vector", "p1", gc.v["p1"])
                            def vct(v):
                                v.tensor_copy(out=ctrow[0:1, 0:512], in_=ps_h0[0][0:1, :])
                                v.tensor_copy(out=ctrow[0:1, 512:1024], in_=ps_h0[1][0:1, :])
                            gc.run("vector", vct)
                            gc.dinc("vector", "v1", 1)

                            gc.wait("tensor", "v1", gc.v["v1"])
                            def tcb(t):
                                for nh in range(2):
                                    mm = t.matmul(out=ps_h1[nh][:], lhsT=cb_sb[0:1, 128:256],
                                                  rhs=ctrow[0:1, nh * 512:(nh + 1) * 512],
                                                  start=True, stop=True)
                                return mm
                            gc.tinc("tensor", tcb, "p1", 1)

                            gc.wait("vector", "p1", gc.v["p1"])
                            def vcb(v, e=e):
                                for nh in range(2):
                                    inst = v.tensor_copy(out=cball[:, e * 1024 + nh * 512: e * 1024 + (nh + 1) * 512],
                                                         in_=ps_h1[nh][:])
                                return inst
                            gc.tinc("vector", vcb, "v1", 1)

                        CBV = gc.v["v1"]

                        SCB = gc.v["v1"]
                        for ch in range(16):
                            half, chh = ch // 8, ch % 8
                            if chh == 0:
                                gc.wait("gpsimd", "pk", PKV[0])
                                if CHP[0]:
                                    gc.wait("gpsimd", "p2", CHP[0])
                                for q2 in range(2):
                                    gc.dma("gpsimd", "lx",
                                           out=rcp2[:].rearrange("o (hp x) -> o hp x", hp=4)
                                           [:, :, q2 * 512:(q2 + 1) * 512],
                                           in_=rcp_park[q2 * 16:q2 * 16 + 16]
                                           .rearrange("(o hp) t -> o hp t", o=2)
                                           [:, 4 * half:4 * (half + 1), :])
                                gc.wait("tensor", "lx", gc.v["lx"])
                                gc.wait("tensor", "wtb", TBLW)
                            if ch >= 2:
                                gc.wait("tensor", "v1", SCB + ch - 1)
                            else:
                                gc.wait("tensor", "v1", CBV)
                            def tbb(t, ch=ch, chh=chh):
                                return t.matmul(out=ps_h0[ch % 2][:],
                                                lhsT=cb_sb[0:2, 384:512],
                                                rhs=rcp2[0:2, chh * 512:(chh + 1) * 512],
                                                start=True, stop=True)
                            gc.tinc("tensor", tbb, "p2", 1)
                            CHP[0] = gc.v["p2"]
                            gc.wait("vector", "p2", gc.v["p2"])
                            def vsc(v, ch=ch):
                                return v.tensor_tensor(out=act[:, ch * 512:(ch + 1) * 512],
                                                       in0=act[:, ch * 512:(ch + 1) * 512],
                                                       in1=ps_h0[ch % 2][:], op=ALU.mult)
                            gc.tinc("vector", vsc, "v1", 1)
                        SCALED = gc.v["v1"]
                        ST0P[0] = gc.v["lx"]

                        lxv5 = {}
                        for tt in range(8):
                            if outev_done[tt % 2]:
                                gc.wait("sync", "v1", outev_done[tt % 2])
                            sn = "wx0" if tt % 2 == 0 else "wx1"
                            gc.dma("sync", sn, out=x_tt[tt % 2][:],
                                   in_=x[tb * TB + tt * 128: tb * TB + (tt + 1) * 128, :])
                            lxv5[tt] = (sn, gc.v[sn])

                            if tt == 0:
                                gc.wait("tensor", "v1", SCALED)
                                gc.wait("tensor", "wtb", TBLW)
                                if SO_STORE[0]:
                                    gc.wait("tensor", "so", SO_STORE[0])
                            gc.wait("tensor", lxv5[tt][0], lxv5[tt][1])
                            if outev_done[tt % 2]:
                                gc.wait("tensor", "v1", outev_done[tt % 2])
                            def two(t, tb=tb, tt=tt):
                                for nh in range(2):
                                    for cc in range(8):
                                        t.matmul(out=ps_o[tt % 2][:, nh * 512:(nh + 1) * 512],
                                                 lhsT=act[:, cc * 1024 + tt * 128: cc * 1024 + (tt + 1) * 128],
                                                 rhs=wo_sb[:, cc * 1024 + nh * 512: cc * 1024 + (nh + 1) * 512],
                                                 start=(cc == 0), stop=False)
                                    mm = t.matmul(out=ps_o[tt % 2][:, nh * 512:(nh + 1) * 512],
                                                  lhsT=cb_sb[0:1, 128:256],
                                                  rhs=bo_row[0:1, nh * 512:(nh + 1) * 512],
                                                  start=False, stop=True)
                                return mm
                            gc.tinc("tensor", two, "p1", 1)

                            gc.wait("vector", "p1", gc.v["p1"])
                            def vwo(v, tt=tt):
                                for nh in range(2):
                                    inst = v.scalar_tensor_tensor(
                                        out=out_sb[:, tt * 1024 + nh * 512: tt * 1024 + (nh + 1) * 512],
                                        in0=ps_o[tt % 2][:, nh * 512:(nh + 1) * 512],
                                        scalar=1.0, in1=x_tt[tt % 2][:, nh * 512:(nh + 1) * 512],
                                        op0=ALU.mult, op1=ALU.add)
                                return inst
                            gc.tinc("vector", vwo, "v1", 1)
                            outev_done[tt % 2] = gc.v["v1"]
                        WOEND = gc.v["p1"]

                        if debug and tb == 0:
                            gc.wait("sync", "v1", gc.v["v1"])
                            gc.dma("sync", "so", out=outinit_dbg, in_=out_sb[:])
                            gc.dma("sync", "so", out=cball_dbg, in_=cball[:])
                            gc.dma("sync", "so", out=aostg_dbg, in_=act[:, 0:8192])

                        for e in range(E):
                            for hcg in range(4):
                                gw2 = gstr_done[hcg % 2]
                                if gw2 != 0:
                                    gc.wait("sync", "s2", gw2[0])
                                    gc.wait("sync", "v2", gw2[1])
                                if hcg == 0:
                                    gc.wait("gpsimd", "p1", gc.v["p1"])
                                wsn = "wg0" if hcg % 2 == 0 else "wg1"
                                gc.dma("sync", wsn, out=g0b[hcg % 2][:],
                                       in_=g0t_s[e, hcg * 4:(hcg + 1) * 4].rearrange("h p c -> p h c"))
                                gc.dma("sync", wsn, out=g1b[hcg % 2][:],
                                       in_=g1t_s[e, hcg * 4:(hcg + 1) * 4].rearrange("h p c -> p h c"))
                                gc.dma("gpsimd", "lg",
                                       out=lw_e[:, hcg * 4096:(hcg + 1) * 4096],
                                       in_=lw[e, hcg * 512:(hcg + 1) * 512, :].rearrange("(h p) c -> p h c", p=128))
                                LWV = gc.v["lg"]
                                GLW = (wsn, gc.v[wsn])

                                for hc4 in range(4):
                                    hc = hcg * 4 + hc4
                                    par = hcg % 2
                                    for ts2 in range(2):
                                        if e == 0 and hcg == 0 and hc4 == 0:
                                            gc.wait("tensor", "v1", SCALED)
                                        gc.wait("tensor", GLW[0], GLW[1])
                                        if gelu_done[ts2]:
                                            gc.wait("tensor", "s2", gelu_done[ts2])
                                        def tg0(t, hc4=hc4, ts2=ts2, par=par):
                                            for dc in range(8):
                                                mm = t.matmul(out=ps_h0[ts2][:],
                                                              lhsT=g0b[par][:, hc4 * 1024 + dc * 128: hc4 * 1024 + (dc + 1) * 128],
                                                              rhs=xn_tb[:, dc * 1024 + ts2 * 512: dc * 1024 + (ts2 + 1) * 512],
                                                              start=(dc == 0), stop=(dc == 7))
                                            return mm
                                        gc.tinc("tensor", tg0, "p2", 1)
                                        H0P = gc.v["p2"]
                                        if act_done[ts2]:
                                            gc.wait("tensor", "v2", act_done[ts2])
                                        def tg1(t, hc4=hc4, ts2=ts2, par=par):
                                            for dc in range(8):
                                                mm = t.matmul(out=ps_h1[ts2][:],
                                                              lhsT=g1b[par][:, hc4 * 1024 + dc * 128: hc4 * 1024 + (dc + 1) * 128],
                                                              rhs=xn_tb[:, dc * 1024 + ts2 * 512: dc * 1024 + (ts2 + 1) * 512],
                                                              start=(dc == 0), stop=(dc == 7))
                                            return mm
                                        gc.tinc("tensor", tg1, "p2", 1)
                                        H1P = gc.v["p2"]

                                        gc.wait("scalar", "p2", H0P)
                                        if act_done[ts2]:
                                            gc.wait("scalar", "v2", act_done[ts2])
                                        gc.tinc("scalar", lambda s, ts2=ts2: s.activation(
                                            out=gel[ts2][:], in_=ps_h0[ts2][:], func=AF.Gelu_apprx_tanh), "s2", 1)
                                        gelu_done[ts2] = gc.v["s2"]

                                        gc.wait("vector", "p2", H1P)
                                        gc.wait("vector", "s2", gc.v["s2"])
                                        if e == 0 and hc < 8:
                                            gc.wait("vector", "p1", WOEND)
                                        def vg(v, e=e, hc=hc, ts2=ts2):
                                            v.tensor_tensor(out=tmp_b[ts2][:], in0=gel[ts2][:],
                                                            in1=ps_h1[ts2][:], op=ALU.mult)
                                            return v.tensor_tensor(out=act[:, hc * 1024 + ts2 * 512: hc * 1024 + (ts2 + 1) * 512],
                                                                   in0=tmp_b[ts2][:],
                                                                   in1=cball[:, e * 1024 + ts2 * 512: e * 1024 + (ts2 + 1) * 512],
                                                                   op=ALU.mult)
                                        gc.tinc("vector", vg, "v2", 1)
                                        act_done[ts2] = gc.v["v2"]
                                gstr_done[hcg % 2] = (gc.v["s2"], gc.v["v2"])
                            GATEV = gc.v["v2"]

                            if debug and tb == 0 and e == 0:
                                gc.wait("sync", "v2", gc.v["v2"])
                                gc.dma("sync", "so", out=act_dbg, in_=act[:])
                                gc.dma("sync", "so", out=gw_dbg[:, 0:4096], in_=g0b[0][:])
                                gc.dma("sync", "so", out=gw_dbg[:, 4096:8192], in_=g0b[1][:])

                            for tt in range(8):
                                if tt == 0:
                                    gc.wait("tensor", "v2", GATEV)
                                    gc.wait("tensor", "lg", gc.v["lg"])
                                if outev_done[tt % 2]:
                                    gc.wait("tensor", "v1", outev_done[tt % 2])
                                def tl(t, tt=tt):
                                    for nh in range(2):
                                        for hc in range(16):
                                            mm = t.matmul(out=ps_o[tt % 2][:, nh * 512:(nh + 1) * 512],
                                                          lhsT=act[:, hc * 1024 + tt * 128: hc * 1024 + (tt + 1) * 128],
                                                          rhs=lw_e[:, hc * 1024 + nh * 512: hc * 1024 + (nh + 1) * 512],
                                                          start=(hc == 0), stop=(hc == 15))
                                    return mm
                                gc.tinc("tensor", tl, "p1", 1)

                                gc.wait("vector", "p1", gc.v["p1"])
                                def vl(v, tt=tt):
                                    for nh in range(2):
                                        sl = slice(tt * 1024 + nh * 512, tt * 1024 + (nh + 1) * 512)
                                        inst = v.tensor_tensor(out=out_sb[:, sl],
                                                               in0=ps_o[tt % 2][:, nh * 512:(nh + 1) * 512],
                                                               in1=out_sb[:, sl], op=ALU.add)
                                    return inst
                                gc.tinc("vector", vl, "v1", 1)
                                outev_done[tt % 2] = gc.v["v1"]

                        if debug and tb == 0 and e == E - 1:
                            pass
                        if debug and tb == 0:
                            gc.wait("sync", "v1", gc.v["v1"])
                            gc.dma("sync", "so", out=oute0_dbg, in_=out_sb[:])
                        gc.wait("sync", "v1", gc.v["v1"])
                        gc.dma("sync", "so",
                               out=out[tb * TB:(tb + 1) * TB, :].rearrange("(tt p) c -> p tt c", p=128),
                               in_=out_sb[:])
                        SO_STORE[0] = gc.v["so"]

                multi_emit(nc, blk, sems, V, gen5)

                def f_fin(s):
                    s.wait_ge(sems["so"], p.v["so"])
                blk.sync(f_fin)

    return nc



def _consts():
    ident_b = np.eye(128, dtype=ml_dtypes.bfloat16)
    ones_b = np.ones((128, 128), dtype=ml_dtypes.bfloat16)
    kk0, qq0 = np.meshgrid(np.arange(128), np.arange(128), indexing="ij")
    tri01 = (kk0 <= qq0).astype(ml_dtypes.bfloat16)
    blkpat = np.zeros((128, 128), dtype=ml_dtypes.bfloat16)
    blkpat[0, 0:64] = 1
    blkpat[1, 64:128] = 1
    cstb = np.concatenate([ident_b, ones_b, tri01, blkpat], axis=1)
    ident_f = np.eye(128, dtype=np.float32)
    kk, qq = np.meshgrid(np.arange(128), np.arange(128), indexing="ij")
    maskT = np.where(kk <= qq, 0.0, BIGNEG).astype(np.float32)
    ones_f = np.ones((128, 128), dtype=np.float32)
    cstf = np.concatenate([ident_f, maskT, ones_f], axis=1)
    return cstb, cstf


def make_in_maps(inputs):
    cstb, cstf = _consts()
    xfull = np.ascontiguousarray(np.asarray(inputs["L0_attn_x"], dtype=np.float32))
    shared = dict(
        wq=np.ascontiguousarray(inputs["L0_attn_Wq"], dtype=np.float32),
        bq=np.ascontiguousarray(inputs["L0_attn_bq"], dtype=np.float32),
        wkd=np.ascontiguousarray(inputs["L0_attn_Wkd"], dtype=np.float32),
        bkd=np.ascontiguousarray(inputs["L0_attn_bkd"], dtype=np.float32),
        wku=np.ascontiguousarray(inputs["L0_attn_Wku"], dtype=np.float32),
        bku=np.ascontiguousarray(inputs["L0_attn_bku"], dtype=np.float32),
        wo=np.ascontiguousarray(inputs["L0_attn_Wo"], dtype=np.float32),
        bo=np.ascontiguousarray(inputs["L0_attn_bo"], dtype=np.float32),
        rw=np.ascontiguousarray(inputs["L0_ffn_router_w"], dtype=np.float32),
        rsc=np.ascontiguousarray(inputs["L0_ffn_router_scale"], dtype=np.float32),
        pes=np.ascontiguousarray(inputs["L0_ffn_per_expert_scale"], dtype=np.float32),
        gw=np.ascontiguousarray(inputs["L0_ffn_gating_w"], dtype=np.float32),
        lw=np.ascontiguousarray(inputs["L0_ffn_linear_w"], dtype=np.float32),
        cstb=cstb, cstf=cstf,
    )
    maps = []
    for i in range(NCORE):
        m = dict(shared)
        m["x"] = np.ascontiguousarray(xfull[SEQ * i:SEQ * (i + 1)].reshape(NT, C))
        maps.append(m)
    return maps


def kernel(**inputs):
    nc = build_program()
    maps = make_in_maps(inputs)
    res = run_bass_kernel_spmd(nc, maps, core_ids=list(range(NCORE)))
    outs = [r["out"].reshape(SEQ, T, C) for r in res.results]
    return np.concatenate(outs, axis=0).astype(np.float32)
